# version 02 (final)
# baseline (speedup 1.0000x reference)
import numpy as np
import ml_dtypes
from contextlib import ExitStack

import concourse.bass as bass
import concourse.mybir as mybir
from concourse.bass_utils import run_bass_kernel_spmd

F32 = mybir.dt.float32
BF16 = mybir.dt.bfloat16
AF = mybir.ActivationFunctionType
ALU = mybir.AluOpType
AX = mybir.AxisListType.X

T = 4096
D = 1024
NCORES = 8
BF = ml_dtypes.bfloat16

BF16_DRAM = {
    "l0f_w1", "l0f_w2", "l0f_h1", "l0f_h2", "l0f_w3",
    "l2f_w1", "l2f_w2",
    "l3f_w1", "l3f_w2", "l3f_w3", "l3f_rw",
    "l1f_wqkv", "l1f_wo", "l1f_w2", "l1f_w1a", "l1f_w1g", "l1f_xpos",
    "l0a_wq", "l0a_wk", "l0a_wv", "l0a_wo",
    "l1a_wq", "l1a_wkv", "l1a_wo",
    "l2a_wi", "l2a_wq", "l2a_wk", "l2a_wv", "l2a_wp", "l2a_w1", "l2a_w2",
    "l2a_wo", "l2a_wh2",
    "l3a_wq", "l3a_wk", "l3a_wv", "l3a_wo",
}



class Sem:
    __slots__ = ("h", "n", "name")

    def __init__(self, h, name):
        self.h = h
        self.n = 0
        self.name = name


class C:

    def __init__(self, nc, stack):
        self.nc = nc
        self.stack = stack
        self.pe = nc.tensor
        self.act = nc.scalar
        self.dve = nc.vector
        self.pool = nc.gpsimd
        self.sp = nc.sync
        self.sems = {}
        self.log = []
        self.sb = {}
        self.consts = {}

    def psbank(self, i):
        return self.ps8[:, i, :]

    def psbank_bf(self, i):
        return self.ps8[:, i, 0:64].bitcast(BF16)

    def sem(self, name):
        if name not in self.sems:
            h = self.stack.enter_context(self.nc.semaphore(name))
            self.sems[name] = Sem(h, name)
        return self.sems[name]

    def w(self, eng, ename, sem, val=None):
        v = sem.n if val is None else val
        if v <= 0:
            return 0
        eng.wait_ge(sem.h, v)
        self.log.append(("w", ename, sem.name, v))
        return v

    def i(self, instr, ename, sem, k=1):
        instr.then_inc(sem.h, k)
        sem.n += k
        self.log.append(("i", ename, sem.name, k))
        return sem.n

    def dop(self, instr):
        dv = self.sem("dv")
        self.i(instr, "dve", dv, 1)
        self.w(self.dve, "dve", dv)
        return dv.n

    def aop(self, instr):
        acte = self.sem("acte")
        self.i(instr, "act", acte, 1)
        self.w(self.act, "act", acte)
        return acte.n

    def check_deadlock(self):
        qs = {}
        for ent in self.log:
            qs.setdefault(ent[1], []).append(ent)
        pos = {e: 0 for e in qs}
        vals = {s: 0 for s in self.sems}
        progress = True
        while progress:
            progress = False
            for e, q in qs.items():
                while pos[e] < len(q):
                    kind, _, sname, v = q[pos[e]]
                    if kind == "w":
                        if vals[sname] >= v:
                            pos[e] += 1
                            progress = True
                        else:
                            break
                    else:
                        vals[sname] += v
                        pos[e] += 1
                        progress = True
        stuck = {e: q[pos[e]] for e, q in qs.items() if pos[e] < len(q)}
        if stuck:
            lines = [f"  {e}: stuck at {q} (sem values {vals})"
                     for e, q in stuck.items()]
            raise RuntimeError("semaphore deadlock detected:\n" + "\n".join(lines))


def bcast_pp(x, n=128):
    x = np.asarray(x, dtype=np.float32).reshape(1, -1)
    return np.ascontiguousarray(np.broadcast_to(x, (n, x.shape[1])))


def scales_tile(s):
    s = np.asarray(s, dtype=np.float32)
    m = s.shape[0]
    return np.ascontiguousarray(s.reshape(m // 128, 128).T)


def wT_ogtiles(w_T):
    w_T = np.asarray(w_T, dtype=np.float32)
    K, M = w_T.shape
    return np.ascontiguousarray(w_T.reshape(K, M // 128, 128).transpose(1, 0, 2))




def emit_consts(c, io):
    xl = c.sem("cst")
    ident = c.nc.alloc_sbuf_tensor("ident_bf", [128, 128], BF16).ap()
    c.consts["ident_bf"] = ident
    c.i(c.pool.dma_start(out=ident, in_=io["ident"]), "pool", xl, 16)
    onesrow = c.nc.alloc_sbuf_tensor("ones_row_sb", [1, 128], F32).ap()
    c.consts["ones_row"] = onesrow
    c.i(c.pool.dma_start(out=onesrow, in_=io["ones_row"]), "pool", xl, 16)
    identf = c.nc.alloc_sbuf_tensor("ident_f32", [128, 128], F32).ap()
    c.consts["ident_f32"] = identf
    c.i(c.pool.dma_start(out=identf, in_=io["ident"]), "pool", xl, 16)
    c.consts["_wl_done"] = xl.n


def emit_ln_fm_quarter(c, xin, q, ln_fm, bufs, ntiles=8, global_dst=False):
    dv = c.sem("dv")
    pes = c.sem("pe")
    acte = c.sem("acte")
    sx = c.sem("sx")
    ident = c.consts["ident_bf"]
    stat = bufs["stat"]
    s_sum, s_sq = stat[:, 0:1], stat[:, 1:2]
    s_mean, s_m2 = stat[:, 2:3], stat[:, 3:4]
    s_bias, s_std, s_rstd = stat[:, 4:5], stat[:, 5:6], stat[:, 6:7]
    for i in range(ntiles):
        tt = q * ntiles + i
        xt = bufs["xt"][i % 2]
        lt = bufs["lt"][i % 2]
        xsem = c.sem(f"xt{i % 2}")
        c.w(c.sp, "sp", sx)
        c.w(c.sp, "sp", dv, c.sems["dv"].n)
        c.w(c.sp, "sp", acte, c.sems["acte"].n)
        c.i(c.sp.dma_start(out=xt, in_=xin[tt * 128:(tt + 1) * 128, :]),
            "sp", xsem, 16)
        c.w(c.dve, "dve", xsem)
        c.w(c.dve, "dve", acte)
        c.w(c.dve, "dve", pes)
        c.dop(c.dve.reduce_sum(s_sum, xt, axis=AX))
        c.dop(c.dve.tensor_scalar_mul(s_mean, s_sum, 1.0 / D))
        c.dop(c.dve.tensor_tensor(s_m2, s_mean, s_mean, ALU.mult))
        c.i(c.dve.tensor_scalar(s_bias, s_m2, -1.0, 1e-5, ALU.mult, ALU.add),
            "dve", dv, 1)
        c.w(c.act, "act", xsem)
        c.w(c.act, "act", dv)
        c.aop(c.act.activation(bufs["sq"], xt, AF.Square, accum_out=s_sq))
        c.i(c.act.activation(s_std, s_sq, AF.Sqrt, bias=s_bias, scale=1.0 / D),
            "act", acte, 1)
        c.w(c.dve, "dve", acte)
        c.dop(c.dve.reciprocal(s_rstd, s_std))
        c.i(c.dve.scalar_tensor_tensor(
            lt, xt, s_mean, s_rstd.to_broadcast([128, D]),
            ALU.subtract, ALU.mult), "dve", dv, 1)
        c.w(c.pe, "pe", c.sem("cst"))
        c.w(c.pe, "pe", dv)
        for f in range(8):
            tp = bufs["tp"][f % 2]
            c.w(c.pe, "pe", acte)
            c.i(c.pe.transpose(tp, lt[:, f * 128:(f + 1) * 128], ident),
                "pe", pes, 1)
            c.w(c.act, "act", pes)
            col = tt * 128 if global_dst else i * 128
            c.i(c.act.copy(ln_fm[:, f, col:col + 128], tp),
                "act", acte, 1)


def emit_linear_fm(c, psname, K, M, T2, w_dram, in_slices, out_slices,
                   scale_ap_fn=None, act_fn=AF.Copy, wbufs=None, wsems=None,
                   bias_ap_fn=None, extra_pe_waits=()):
    pes = c.sem("pe")
    acte = c.sem("acte")
    nkc, ntb, nog = K // 128, T2 // 512, M // 128
    psum = c.sb[psname]
    pe_done = {}
    evac_done = {}
    c.w(c.pe, "pe", acte)
    for eng, ename, sem, val in extra_pe_waits:
        c.w(eng, ename, sem, val)
    pe_entry = c.sems["pe"].n
    for og in range(nog):
        wb = wbufs[og % 2]
        ws = wsems[og % 2]
        if og >= 2:
            c.w(c.pool, "pool", pes, pe_done[og - 2])
        else:
            c.w(c.pool, "pool", pes, pe_entry)
        src = w_dram[og].rearrange("(kc p) o -> p kc o", p=128)
        c.i(c.pool.dma_start(out=wb[:, 0:nkc, :], in_=src), "pool", ws, 16)
        c.w(c.pe, "pe", ws)
        if og >= 2:
            c.w(c.pe, "pe", acte, evac_done[og - 2])
        bs = og % 2
        last = None
        for kc in range(nkc):
            for tb in range(ntb):
                last = c.pe.matmul(psum[:, bs, tb, :], wb[:, kc, :],
                                   in_slices(kc, tb),
                                   start=(kc == 0), stop=(kc == nkc - 1))
        c.i(last, "pe", pes, 1)
        pe_done[og] = pes.n
        c.w(c.act, "act", pes, pe_done[og])
        for tb in range(ntb):
            sc = scale_ap_fn(og) if scale_ap_fn is not None else 1.0
            bi = bias_ap_fn(og) if bias_ap_fn is not None else 0.0
            fn = act_fn
            if bias_ap_fn is not None and fn == AF.Copy:
                fn = AF.Identity
            c.i(c.act.activation(out_slices(og, tb), psum[:, bs, tb, :],
                                 fn, scale=sc, bias=bi), "act", acte, 1)
        evac_done[og] = acte.n


def emit_ab_fused(c, psname, K, M, T2, wA, wB, in_slices, out_slices,
                  sA_fn, sB_fn, actA, wbufsA, wbufsB, wsemsA, wsemsB,
                  tmpA, tmpB, extra_pe_waits=(), biasA_fn=None, biasB_fn=None):
    pes = c.sem("pe")
    acte = c.sem("acte")
    dv = c.sem("dv")
    nkc, ntb, nog = K // 128, T2 // 512, M // 128
    psum = c.sb[psname]
    pe_done, evac_done = {}, {}
    c.w(c.pe, "pe", acte)
    for eng, ename, sem, val in extra_pe_waits:
        c.w(eng, ename, sem, val)
    pe_entry = c.sems["pe"].n
    for og in range(nog):
        wa, wb_ = wbufsA[og % 2], wbufsB[og % 2]
        if og >= 2:
            c.w(c.pool, "pool", pes, pe_done[og - 2])
        else:
            c.w(c.pool, "pool", pes, pe_entry)
        c.i(c.pool.dma_start(out=wa[:, 0:nkc, :],
                             in_=wA[og].rearrange("(kc p) o -> p kc o", p=128)),
            "pool", wsemsA[og % 2], 16)
        c.i(c.pool.dma_start(out=wb_[:, 0:nkc, :],
                             in_=wB[og].rearrange("(kc p) o -> p kc o", p=128)),
            "pool", wsemsB[og % 2], 16)
        c.w(c.pe, "pe", wsemsA[og % 2])
        c.w(c.pe, "pe", wsemsB[og % 2])
        if og >= 1:
            c.w(c.pe, "pe", acte, evac_done[og - 1])
        last = None
        for kc in range(nkc):
            for tb in range(ntb):
                last = c.pe.matmul(psum[:, 0, tb, :], wa[:, kc, :],
                                   in_slices(kc, tb),
                                   start=(kc == 0), stop=(kc == nkc - 1))
        for kc in range(nkc):
            for tb in range(ntb):
                last = c.pe.matmul(psum[:, 1, tb, :], wb_[:, kc, :],
                                   in_slices(kc, tb),
                                   start=(kc == 0), stop=(kc == nkc - 1))
        c.i(last, "pe", pes, 1)
        pe_done[og] = pes.n
        c.w(c.act, "act", pes, pe_done[og])
        c.w(c.act, "act", dv)
        ta, tb2 = tmpA[og % 2], tmpB[og % 2]
        for tb in range(ntb):
            bA = biasA_fn(og) if biasA_fn is not None else 0.0
            bB = biasB_fn(og) if biasB_fn is not None else 0.0
            fB = AF.Identity if biasB_fn is not None else AF.Copy
            c.i(c.act.activation(ta[:, tb * 512:(tb + 1) * 512],
                                 psum[:, 0, tb, :], actA, scale=sA_fn(og),
                                 bias=bA), "act", acte, 1)
            c.i(c.act.activation(tb2[:, tb * 512:(tb + 1) * 512],
                                 psum[:, 1, tb, :], fB, scale=sB_fn(og),
                                 bias=bB), "act", acte, 1)
        evac_done[og] = acte.n
        c.w(c.dve, "dve", acte, evac_done[og])
        c.i(c.dve.tensor_tensor(out_slices(og), ta, tb2, ALU.mult), "dve", dv, 1)


def emit_residual_fm(c, xin, xout, delta_fm, q, bufs, ps_d, ntiles=8):
    dv = c.sem("dv")
    pes = c.sem("pe")
    acte = c.sem("acte")
    sx = c.sem("sx")
    for i in range(ntiles):
        tt = q * ntiles + i
        xo = bufs["xo"][i % 2]
        xsem = c.sem(f"xo{i % 2}")
        c.w(c.sp, "sp", sx)
        c.w(c.sp, "sp", dv, c.sems["dv"].n)
        c.i(c.sp.dma_start(out=xo, in_=xin[tt * 128:(tt + 1) * 128, :]),
            "sp", xsem, 16)
        for f in range(8):
            c.w(c.pe, "pe", acte)
            c.w(c.pe, "pe", dv)
            c.i(c.pe.transpose(ps_d[f % 2], delta_fm[:, f, i * 128:(i + 1) * 128],
                               c.consts["ident_bf"]), "pe", pes, 1)
            c.w(c.dve, "dve", pes)
            if f == 0:
                c.w(c.dve, "dve", xsem)
            c.i(c.dve.tensor_tensor(
                xo[:, f * 128:(f + 1) * 128],
                xo[:, f * 128:(f + 1) * 128], ps_d[f % 2], ALU.add),
                "dve", dv, 1)
        c.w(c.pool, "pool", dv)
        c.i(c.pool.dma_start(out=xout[tt * 128:(tt + 1) * 128, :], in_=xo),
            "pool", sx, 16)




def prep_l0_ffn(inputs):
    return {
        "l0f_w1": wT_ogtiles(inputs["L0_ffn_w1_w"].T),
        "l0f_w2": wT_ogtiles(inputs["L0_ffn_w2_w"].T),
        "l0f_h1": wT_ogtiles(inputs["L0_ffn_h1_w"].T),
        "l0f_h2": wT_ogtiles(inputs["L0_ffn_h2_w"].T),
        "l0f_w3": wT_ogtiles(inputs["L0_ffn_w3_w"].T),
        "l0f_s1": scales_tile(inputs["L0_ffn_w1_s"]),
        "l0f_s2": scales_tile(inputs["L0_ffn_w2_s"]),
        "l0f_sh1": scales_tile(inputs["L0_ffn_h1_s"]),
        "l0f_sh2": scales_tile(inputs["L0_ffn_h2_s"]),
        "l0f_s3": scales_tile(inputs["L0_ffn_w3_s"]),
    }


def decl_l0_ffn(nc):
    io = {}
    for n, shape in [("l0f_w1", [32, 1024, 128]), ("l0f_w2", [32, 1024, 128]),
                     ("l0f_h1", [32, 4096, 128]), ("l0f_h2", [32, 4096, 128]),
                     ("l0f_w3", [8, 4096, 128])]:
        io[n] = nc.dram_tensor(n, shape, F32, kind="ExternalInput").ap()
    for n, m in [("l0f_s1", 32), ("l0f_s2", 32), ("l0f_sh1", 32),
                 ("l0f_sh2", 32), ("l0f_s3", 8)]:
        io[n] = nc.dram_tensor(n, [128, m], F32, kind="ExternalInput").ap()
    return io


def emit_l0_ffn(c, xin, xout, io):
    nc = c.nc
    stk = ExitStack()
    cst = c.sem("cst_l0f")
    dv = c.sem("dv")
    acte = c.sem("acte")
    wsA = [c.sem("wA0"), c.sem("wA1")]
    wsB = [c.sem("wB0"), c.sem("wB1")]
    wsC = [c.sem("wC0"), c.sem("wC1")]
    sb = lambda n, shp, dt: stk.enter_context(nc.sbuf_tensor(n, shp, dt)).ap()
    T2 = 512
    ln_fm = sb("l0f_ln", [128, 8, T2], BF16)
    R0 = sb("l0f_R0", [128, 32, T2], BF16)
    R1 = sb("l0f_R1", [128, 32, T2], BF16)
    tmpA = [sb(f"l0f_tA{i}", [128, T2], BF16) for i in range(2)]
    tmpB = [sb(f"l0f_tB{i}", [128, T2], BF16) for i in range(2)]
    delta = sb("l0f_delta", [128, 8, T2], BF16)
    scales = {}
    for n, m in [("s1", 32), ("s2", 32), ("sh1", 32), ("sh2", 32), ("s3", 8)]:
        scales[n] = sb(f"sb_l0f_{n}", [128, m], F32)
        c.i(c.pool.dma_start(out=scales[n], in_=io[f"l0f_{n}"]), "pool", cst, 16)
    wb_small = [sb(f"l0f_wbs{i}", [128, 8, 128], BF16) for i in range(2)]
    wb_small2 = [sb(f"l0f_wbs2{i}", [128, 8, 128], BF16) for i in range(2)]
    wb_big = [sb(f"l0f_wbb{i}", [128, 32, 128], BF16) for i in range(2)]
    bufs = c.bufs
    c.sb["ps_l0f"] = c.ps8[:, 0:2, :].rearrange("p (a b) f -> p a b f", b=1)
    ps_d = [c.psbank_bf(4), c.psbank_bf(5)]
    sl = lambda buf: (lambda kc, tb: buf[:, kc, tb * 512:(tb + 1) * 512])
    st = lambda buf: (lambda og, tb: buf[:, og, tb * 512:(tb + 1) * 512])

    for q in range(8):
        emit_ln_fm_quarter(c, xin, q, ln_fm, bufs, ntiles=4)
        emit_ab_fused(c, "ps_l0f", 1024, 4096, T2,
                      io["l0f_w1"], io["l0f_w2"], sl(ln_fm),
                      lambda og: R0[:, og, :],
                      lambda og: scales["s1"][:, og:og + 1],
                      lambda og: scales["s2"][:, og:og + 1],
                      AF.Silu, wb_small, wb_small2, wsA, wsB, tmpA, tmpB,
                      extra_pe_waits=[(c.pe, "pe", cst, cst.n)])
        dv_h = dv.n
        emit_linear_fm(c, "ps_l0f", 4096, 4096, T2, io["l0f_h1"],
                       sl(R0), st(R1),
                       scale_ap_fn=lambda og: scales["sh1"][:, og:og + 1],
                       act_fn=AF.Gelu, wbufs=wb_big, wsems=wsC,
                       extra_pe_waits=[(c.pe, "pe", dv, dv_h)])
        emit_linear_fm(c, "ps_l0f", 4096, 4096, T2, io["l0f_h2"],
                       sl(R1), st(R0),
                       scale_ap_fn=lambda og: scales["sh2"][:, og:og + 1],
                       act_fn=AF.Gelu, wbufs=wb_big, wsems=wsC)
        emit_linear_fm(c, "ps_l0f", 4096, 1024, T2, io["l0f_w3"],
                       sl(R0), st(delta),
                       scale_ap_fn=lambda og: scales["s3"][:, og:og + 1],
                       act_fn=AF.Copy, wbufs=wb_big, wsems=wsC)
        emit_residual_fm(c, xin, xout, delta, q, bufs, ps_d, ntiles=4)
    stk.close()




def emit_attn_hqb(c, env, kt_fn, qt_ap, v_ap_fn, nkc, mask_fn, scale,
                  dst_rows, qs, par, qkv_done):
    dv = c.sem("dv")
    pes = c.sem("pe")
    acte = c.sem("acte")
    ps_s, ps_o, ps_b = env["ps_s"], env["ps_o"], env["ps_b"]
    pb, sbo, stp = env["pb"], env["sbo"], env["stp"]
    cm, tcm, negc1, negc = env["cm"], env["tcm"], env["negc1"], env["negc"]
    rd, sbb = env["rd"], env["sbb"]
    ones_row = c.consts["ones_row"]
    identf = c.consts["ident_f32"]
    c.w(c.pe, "pe", acte, qkv_done)
    red_done = {}
    for kc in range(nkc):
        if kc >= 2:
            c.w(c.pe, "pe", dv, red_done[kc - 2])
        c.i(c.pe.matmul(ps_s[kc % 2], kt_fn(kc), qt_ap, start=True, stop=True),
            "pe", pes, 1)
        c.w(c.dve, "dve", pes)
        c.dop(c.dve.reduce_max(tcm, ps_s[kc % 2], axis=AX))
        if kc == 0:
            c.dop(c.dve.tensor_copy(cm, tcm))
        else:
            c.dop(c.dve.tensor_max(cm, cm, tcm))
        red_done[kc] = c.sems["dv"].n
    c.w(c.pe, "pe", dv)
    c.i(c.pe.transpose(ps_b[0:1, 0:128], cm, identf), "pe", pes, 1)
    c.w(c.dve, "dve", pes)
    c.i(c.dve.reduce_max(negc1, ps_b[0:1, 0:128], axis=AX, negate=True),
        "dve", dv, 1)
    c.w(c.pe, "pe", dv)
    c.i(c.pe.matmul(ps_b[:, 0:1], ones_row, negc1, start=True, stop=True),
        "pe", pes, 1)
    c.w(c.dve, "dve", pes)
    c.i(c.dve.tensor_scalar_mul(negc, ps_b[:, 0:1], scale), "dve", dv, 1)
    negc_done = c.sems["dv"].n
    exp_done = {}
    av_done = {}
    for kc in range(nkc):
        if kc >= 2:
            c.w(c.pe, "pe", acte, exp_done[kc - 2])
        c.i(c.pe.matmul(ps_s[kc % 2], kt_fn(kc), qt_ap, start=True, stop=True),
            "pe", pes, 1)
        sc_done = c.sems["pe"].n
        mask = mask_fn(kc)
        if mask is not None:
            c.w(c.dve, "dve", pes, sc_done)
            c.i(c.dve.tensor_tensor(ps_s[kc % 2], ps_s[kc % 2], mask, ALU.add),
                "dve", dv, 1)
            c.w(c.act, "act", dv)
        else:
            c.w(c.act, "act", pes, sc_done)
        if kc == 0:
            c.w(c.act, "act", dv, negc_done)
        if kc >= 2:
            c.w(c.act, "act", pes, av_done[kc - 2])
        c.i(c.act.activation(pb[kc % 2], ps_s[kc % 2], AF.Exp,
                             bias=negc, scale=scale), "act", acte, 1)
        exp_done[kc] = c.sems["acte"].n
        c.w(c.pe, "pe", acte, exp_done[kc])
        if kc == 0:
            c.w(c.pe, "pe", dv)
        c.i(c.pe.matmul(ps_o[0:65, :], v_ap_fn(kc), pb[kc % 2],
                        start=(kc == 0), stop=(kc == nkc - 1)), "pe", pes, 1)
        av_done[kc] = c.sems["pe"].n
    c.w(c.dve, "dve", pes, av_done[nkc - 1])
    c.w(c.dve, "dve", stp[par])
    c.i(c.dve.reciprocal(rd, ps_o[64:65, :]), "dve", dv, 1)
    c.w(c.pe, "pe", dv)
    c.i(c.pe.matmul(ps_b[0:64, :], ones_row[0:1, 0:64], rd,
                    start=True, stop=True), "pe", pes, 1)
    c.w(c.act, "act", pes)
    c.i(c.act.copy(sbb, ps_b[0:64, :]), "act", acte, 1)
    c.w(c.dve, "dve", acte)
    c.i(c.dve.tensor_tensor(sbo[par], ps_o[0:64, :], sbb, ALU.mult),
        "dve", dv, 1)
    c.w(c.pool, "pool", dv)
    c.i(c.pool.dma_start(out=env["op_dram"][dst_rows, qs], in_=sbo[par]),
        "pool", stp[par], 16)


def emit_outproj_residual(c, env, w_rhs_fn, bo_tile, xin, xout, opb, opl,
                          nkc, add_bias, extra_pe_sems=()):
    dv = c.sem("dv")
    pes = c.sem("pe")
    sx = c.sem("sx")
    stp = env["stp"]
    ps_d = env["ps_d"]
    op_fm = env["op_dram"].rearrange("(kc p) t -> p kc t", p=128)
    for qb in range(8):
        c.w(c.sp, "sp", stp[0])
        c.w(c.sp, "sp", stp[1])
        c.w(c.sp, "sp", pes, c.sems["pe"].n)
        c.i(c.sp.dma_start(out=opb[qb % 2][:, 0:nkc, :],
                           in_=op_fm[:, :, qb * 512:(qb + 1) * 512]),
            "sp", opl[qb % 2], 16)
        for ttl in range(4):
            tt = qb * 4 + ttl
            xo = c.bufs["xo"][ttl % 2]
            xsem = c.sem(f"xo{ttl % 2}")
            c.w(c.sp, "sp", sx)
            c.w(c.sp, "sp", dv, c.sems["dv"].n)
            c.i(c.sp.dma_start(out=xo, in_=xin[tt * 128:(tt + 1) * 128, :]),
                "sp", xsem, 16)
            for db in range(2):
                c.w(c.pe, "pe", opl[qb % 2])
                c.w(c.pe, "pe", c.sem("cst"))
                for s_ in extra_pe_sems:
                    c.w(c.pe, "pe", s_)
                c.w(c.pe, "pe", dv, c.sems["dv"].n)
                last = None
                for kc in range(nkc):
                    last = c.pe.matmul(
                        ps_d[db],
                        opb[qb % 2][:, kc, ttl * 128:(ttl + 1) * 128],
                        w_rhs_fn(kc, db), start=(kc == 0), stop=(kc == nkc - 1))
                c.i(last, "pe", pes, 1)
            mm_done = c.sems["pe"].n
            c.w(c.dve, "dve", xsem)
            c.w(c.dve, "dve", pes, mm_done)
            if add_bias:
                c.dop(c.dve.tensor_tensor(xo, xo, bo_tile, ALU.add))
            c.i(c.dve.tensor_tensor(xo[:, 0:512], xo[:, 0:512], ps_d[0],
                                    ALU.add), "dve", dv, 1)
            c.i(c.dve.tensor_tensor(xo[:, 512:1024], xo[:, 512:1024], ps_d[1],
                                    ALU.add), "dve", dv, 1)
            c.w(c.pool, "pool", dv)
            c.i(c.pool.dma_start(out=xout[tt * 128:(tt + 1) * 128, :], in_=xo),
                "pool", sx, 16)




def prep_l0_attn(inputs):
    Wq = np.asarray(inputs["L0_attn_Wq"], dtype=np.float32)
    Wk = np.asarray(inputs["L0_attn_Wk"], dtype=np.float32)
    Wv = np.asarray(inputs["L0_attn_Wv"], dtype=np.float32)
    Wo = np.asarray(inputs["L0_attn_Wo"], dtype=np.float32)
    bo = np.asarray(inputs["L0_attn_bo"], dtype=np.float32)
    return {
        "l0a_wq": wT_ogtiles(Wq),
        "l0a_wk": wT_ogtiles(Wk),
        "l0a_wv": np.ascontiguousarray(Wv.reshape(8, 128, 1024)),
        "l0a_wo": np.ascontiguousarray(Wo.reshape(8, 128, 1024)),
        "l0a_bo": bcast_pp(bo),
    }


def decl_l0_attn(nc):
    io = {}
    io["l0a_wq"] = nc.dram_tensor("l0a_wq", [8, 1024, 128], F32, kind="ExternalInput").ap()
    io["l0a_wk"] = nc.dram_tensor("l0a_wk", [8, 1024, 128], F32, kind="ExternalInput").ap()
    io["l0a_wv"] = nc.dram_tensor("l0a_wv", [8, 128, 1024], F32, kind="ExternalInput").ap()
    io["l0a_wo"] = nc.dram_tensor("l0a_wo", [8, 128, 1024], F32, kind="ExternalInput").ap()
    io["l0a_bo"] = nc.dram_tensor("l0a_bo", [128, 1024], F32, kind="ExternalInput").ap()
    return io


def emit_l0_attn(c, xin, xout, io):
    nc = c.nc
    cst = c.sem("cst_l0a")
    dv = c.sem("dv")
    pes = c.sem("pe")
    acte = c.sem("acte")
    sx = c.sem("sx")
    wsA = [c.sem("wA0"), c.sem("wA1")]
    stp = [c.sem("st0"), c.sem("st1")]
    opl = [c.sem("op0"), c.sem("op1")]
    stk = ExitStack()
    sb = lambda n, shp, dt: stk.enter_context(nc.sbuf_tensor(n, shp, dt)).ap()
    ident = c.consts["ident_bf"]
    identf = c.consts["ident_f32"]
    ones_row = c.consts["ones_row"]

    ln_fm = sb("l0a_ln", [128, 8, T], BF16)
    QT = sb("l0a_qt", [128, T], BF16)
    KT = sb("l0a_kt", [128, T], BF16)
    vg = sb("l0a_vg", [128, 32, 132], BF16)
    pb = [sb(f"l0a_pb{i}", [128, 512], BF16) for i in range(2)]
    dmk = sb("l0a_dmk", [128, 4, 512], F32)
    bo_t = sb("l0a_bo_sb", [128, 1024], F32)
    wob = sb("l0a_wob", [128, 8, 1024], BF16)
    wqb = [sb(f"l0a_wqb{i}", [128, 8, 128], BF16) for i in range(2)]
    wkb = [sb(f"l0a_wkb{i}", [128, 8, 128], BF16) for i in range(2)]
    wvb = [sb(f"l0a_wvb{i}", [128, 8, 128], BF16) for i in range(2)]
    opb = [sb(f"l0a_opb{i}", [128, 8, 512], BF16) for i in range(2)]
    sbo = [sb(f"l0a_sbo{i}", [64, 512], BF16) for i in range(2)]
    cm = sb("l0a_cm", [128, 1], F32)
    tcm = sb("l0a_tcm", [128, 1], F32)
    negc1 = sb("l0a_negc1", [1, 1], F32)
    negc = sb("l0a_negc", [128, 1], F32)
    rd = sb("l0a_rd", [1, 512], F32)
    sbb = sb("l0a_sbb", [64, 512], F32)
    op_dram = nc.dram_tensor("l0a_op", [1024, T], BF16, kind="Internal").ap()

    ps_s = [c.psbank(0), c.psbank(1)]
    ps_o = c.psbank(2)
    ps_b = c.psbank(3)
    ps_d = [c.psbank(4), c.psbank(5)]
    ps_q = c.psbank(0)
    ps_v = c.psbank(1)

    c.i(c.pool.dma_start(out=wob, in_=io["l0a_wo"].rearrange("kc p o -> p kc o")),
        "pool", cst, 16)
    c.i(c.pool.dma_start(out=bo_t, in_=io["l0a_bo"]), "pool", cst, 16)
    c.i(c.pool.dma_start(out=dmk, in_=io["dmask"].rearrange("kb p q -> p kb q")),
        "pool", cst, 16)

    c.w(c.dve, "dve", cst)
    for q in range(8):
        emit_ln_fm_quarter(c, xin, q, ln_fm, c.bufs, ntiles=4, global_dst=True)
    ln_done = c.sems["acte"].n

    for hg in range(8):
        c.w(c.pool, "pool", pes, c.sems["pe"].n)
        c.i(c.pool.dma_start(out=wqb[hg % 2][:, :, :],
                             in_=io["l0a_wq"][hg].rearrange("(kc p) o -> p kc o", p=128)),
            "pool", wsA[hg % 2], 16)
        c.i(c.pool.dma_start(out=wkb[hg % 2][:, :, :],
                             in_=io["l0a_wk"][hg].rearrange("(kc p) o -> p kc o", p=128)),
            "pool", wsA[hg % 2], 16)
        c.i(c.pool.dma_start(out=wvb[hg % 2][:, :, :],
                             in_=io["l0a_wv"][:, :, hg * 128:(hg + 1) * 128]
                             .rearrange("kc p o -> p kc o")),
            "pool", wsA[hg % 2], 16)
        c.w(c.pe, "pe", wsA[hg % 2])
        c.w(c.pe, "pe", acte, ln_done)
        c.w(c.dve, "dve", pes, c.sems["pe"].n)
        c.i(c.dve.memset(vg[:, :, 64:66], 1.0), "dve", dv, 1)
        c.i(c.dve.memset(vg[:, :, 130:132], 1.0), "dve", dv, 1)
        vg_ones = dv.n
        for mat, wb, dst in [("q", wqb[hg % 2], QT), ("k", wkb[hg % 2], KT)]:
            for tb in range(8):
                c.w(c.pe, "pe", acte)
                last = None
                for kc in range(8):
                    last = c.pe.matmul(ps_q, wb[:, kc, :],
                                       ln_fm[:, kc, tb * 512:(tb + 1) * 512],
                                       start=(kc == 0), stop=(kc == 7))
                c.i(last, "pe", pes, 1)
                c.w(c.act, "act", pes)
                c.i(c.act.copy(dst[:, tb * 512:(tb + 1) * 512], ps_q),
                    "act", acte, 1)
        for tt in range(32):
            c.w(c.pe, "pe", acte)
            if tt == 0:
                c.w(c.pe, "pe", dv, vg_ones)
            last = None
            for kc in range(8):
                last = c.pe.matmul(ps_v[:, 0:128],
                                   ln_fm[:, kc, tt * 128:(tt + 1) * 128],
                                   wvb[hg % 2][:, kc, :],
                                   start=(kc == 0), stop=(kc == 7))
            c.i(last, "pe", pes, 1)
            c.w(c.act, "act", pes)
            c.i(c.act.copy(vg[:, tt, 0:64], ps_v[:, 0:64]), "act", acte, 1)
            c.i(c.act.copy(vg[:, tt, 66:130], ps_v[:, 64:128]), "act", acte, 1)
        qkv_done = c.sems["acte"].n

        env = {"ps_s": ps_s, "ps_o": ps_o, "ps_b": ps_b, "ps_d": ps_d,
               "pb": pb, "sbo": sbo, "stp": stp, "cm": cm, "tcm": tcm,
               "negc1": negc1, "negc": negc, "rd": rd, "sbb": sbb,
               "op_dram": op_dram}
        for h2 in range(2):
            base = h2 * 64
            qrows = slice(base, base + 64)
            for qb in range(8):
                nkc = 8 if qb < 2 else 4 * (qb + 1)
                qs = slice(qb * 512, (qb + 1) * 512)
                h = hg * 2 + h2

                def mask_fn(kc, _qb=qb, _nkc=nkc):
                    if _qb >= 2 and kc >= _nkc - 4:
                        return dmk[:, kc - (_nkc - 4), :]
                    return None

                emit_attn_hqb(
                    c, env,
                    lambda kc, _qr=qrows: KT[_qr, kc * 128:(kc + 1) * 128],
                    QT[qrows, qs],
                    lambda kc, _h2=h2: vg[:, kc, _h2 * 66:_h2 * 66 + 65],
                    nkc, mask_fn, 0.125,
                    slice(h * 64, (h + 1) * 64), qs,
                    (hg * 16 + h2 * 8 + qb) % 2, qkv_done)

    emit_outproj_residual(
        c, env,
        lambda kc, db: wob[:, kc, db * 512:(db + 1) * 512],
        bo_t, xin, xout, opb, opl, 8, True)
    stk.close()






def prep_l1_attn(inputs):
    wq = np.asarray(inputs["L1_attn_wq"], dtype=np.float32)
    wk = np.asarray(inputs["L1_attn_wk"], dtype=np.float32)
    wv = np.asarray(inputs["L1_attn_wv"], dtype=np.float32)
    wo = np.asarray(inputs["L1_attn_wo"], dtype=np.float32)
    HD = 64
    fr = 1.0 / (1000.0 ** (np.arange(0, HD, 2) / HD))
    ang = np.arange(T)[:, None] * fr[None, :]
    cs = np.concatenate([np.cos(ang), np.sin(ang)], axis=1)
    wkv = np.concatenate([wk, wv], axis=1)
    return {
        "l1a_wq": np.ascontiguousarray(wq.T.reshape(8, 128, 1024)),
        "l1a_wkv": np.ascontiguousarray(wkv.reshape(8, 128, 128)),
        "l1a_wo": np.ascontiguousarray(wo.T.reshape(8, 128, 1024)),
        "l1a_cs": np.ascontiguousarray(cs.astype(np.float32)),
    }


def decl_l1_attn(nc):
    io = {}
    io["l1a_wq"] = nc.dram_tensor("l1a_wq", [8, 128, 1024], F32, kind="ExternalInput").ap()
    io["l1a_wkv"] = nc.dram_tensor("l1a_wkv", [8, 128, 128], F32, kind="ExternalInput").ap()
    io["l1a_wo"] = nc.dram_tensor("l1a_wo", [8, 128, 1024], F32, kind="ExternalInput").ap()
    io["l1a_cs"] = nc.dram_tensor("l1a_cs", [T, 64], F32, kind="ExternalInput").ap()
    return io


def emit_rope_tm(c, out_bf, qv, cs_c, cs_s, nheads, scratch):
    dv = c.sem("dv")
    qv4 = qv.rearrange("p (h j two) -> p h j two", h=nheads, j=32, two=2)
    E, O = qv4[:, :, :, 0], qv4[:, :, :, 1]
    o2 = out_bf.rearrange("p (h half j) -> p h half j", h=nheads, half=2, j=32)
    cosb = cs_c.unsqueeze(1).to_broadcast([128, nheads, 32])
    sinb = cs_s.unsqueeze(1).to_broadcast([128, nheads, 32])
    sc = scratch.rearrange("p (h j) -> p h j", h=nheads)
    c.dop(c.dve.tensor_tensor(o2[:, :, 0, :], E, cosb, ALU.mult))
    c.dop(c.dve.tensor_tensor(sc, O, sinb, ALU.mult))
    c.dop(c.dve.tensor_tensor(o2[:, :, 0, :], o2[:, :, 0, :], sc, ALU.subtract))
    c.dop(c.dve.tensor_tensor(o2[:, :, 1, :], O, cosb, ALU.mult))
    c.dop(c.dve.tensor_tensor(sc, E, sinb, ALU.mult))
    c.i(c.dve.tensor_tensor(o2[:, :, 1, :], o2[:, :, 1, :], sc, ALU.add),
        "dve", dv, 1)


def emit_l1_attn(c, xin, xout, io):
    nc = c.nc
    cst = c.sem("cst_l1a")
    dv = c.sem("dv")
    pes = c.sem("pe")
    acte = c.sem("acte")
    sx = c.sem("sx")
    stp = [c.sem("st0"), c.sem("st1")]
    opl = [c.sem("op0"), c.sem("op1")]
    stk = ExitStack()
    sb = lambda n, shp, dt: stk.enter_context(nc.sbuf_tensor(n, shp, dt)).ap()
    ident = c.consts["ident_bf"]

    QT = sb("l1a_qt", [128, 8, T], BF16)
    KT2 = sb("l1a_kt", [128, T], BF16)
    vt = sb("l1a_vt", [128, 32, 66], BF16)
    dmk = sb("l1a_dmk", [128, 4, 512], BF16)
    stkA = ExitStack()
    sbA = lambda n, shp, dt: stkA.enter_context(nc.sbuf_tensor(n, shp, dt)).ap()
    ln_fm = sbA("l1a_ln", [128, 8, T], BF16)
    wqb = sbA("l1a_wqb", [128, 8, 1024], BF16)
    wkvb = sbA("l1a_wkvb", [128, 8, 128], BF16)
    csb = sbA("l1a_csb", [128, 32, 64], F32)
    qtmp = sbA("l1a_qtmp", [128, 1024], F32)
    kvtmp = sbA("l1a_kvtmp", [128, 128], F32)
    qr = sbA("l1a_qr", [128, 1024], BF16)
    kr = sbA("l1a_kr", [128, 64], BF16)
    rsc = sbA("l1a_rsc", [128, 512], F32)
    op_dram = nc.dram_tensor("l1a_op", [1024, T], BF16, kind="Internal").ap()

    ps_s = [c.psbank(0), c.psbank(1)]
    ps_o = c.psbank(2)
    ps_b = c.psbank(3)
    ps_d = [c.psbank(4), c.psbank(5)]
    ps_q = [c.psbank(0), c.psbank(1)]
    ps_t = c.psbank_bf(2)

    c.i(c.pool.dma_start(out=wqb, in_=io["l1a_wq"].rearrange("kc p o -> p kc o")),
        "pool", cst, 16)
    c.i(c.pool.dma_start(out=wkvb, in_=io["l1a_wkv"].rearrange("kc p o -> p kc o")),
        "pool", cst, 16)
    c.i(c.pool.dma_start(out=csb, in_=io["l1a_cs"].rearrange("(tt p) j -> p tt j", p=128)),
        "pool", cst, 16)
    c.i(c.pool.dma_start(out=dmk, in_=io["dmask"].rearrange("kb p q -> p kb q")),
        "pool", cst, 16)

    for q in range(8):
        emit_ln_fm_quarter(c, xin, q, ln_fm, c.bufs, ntiles=4, global_dst=True)

    c.w(c.dve, "dve", cst)
    c.i(c.dve.memset(vt[:, :, 64:66], 1.0), "dve", dv, 1)
    vt_ones = dv.n
    c.w(c.pe, "pe", cst)
    for tt in range(32):
        ts_ = slice(tt * 128, (tt + 1) * 128)
        c.w(c.pe, "pe", dv, c.sems["dv"].n)
        for ob in range(2):
            last = None
            for kc in range(8):
                last = c.pe.matmul(ps_q[ob], ln_fm[:, kc, ts_],
                                   wqb[:, kc, ob * 512:(ob + 1) * 512],
                                   start=(kc == 0), stop=(kc == 7))
            c.i(last, "pe", pes, 1)
        last = None
        for kc in range(8):
            last = c.pe.matmul(ps_b[:, 0:128], ln_fm[:, kc, ts_],
                               wkvb[:, kc, :], start=(kc == 0), stop=(kc == 7))
        c.i(last, "pe", pes, 1)
        mm_done = c.sems["pe"].n
        c.w(c.act, "act", pes, mm_done)
        c.aop(c.act.copy(qtmp[:, 0:512], ps_q[0]))
        c.aop(c.act.copy(qtmp[:, 512:1024], ps_q[1]))
        c.aop(c.act.copy(kvtmp, ps_b[:, 0:128]))
        ev_done = c.sems["acte"].n
        c.w(c.dve, "dve", acte, ev_done)
        emit_rope_tm(c, qr, qtmp, csb[:, tt, 0:32], csb[:, tt, 32:64], 16, rsc)
        emit_rope_tm(c, kr, kvtmp[:, 0:64], csb[:, tt, 0:32], csb[:, tt, 32:64],
                     1, rsc[:, 0:32])
        c.i(c.dve.tensor_copy(vt[:, tt, 0:64], kvtmp[:, 64:128]), "dve", dv, 1)
        rope_done = c.sems["dv"].n
        c.w(c.pe, "pe", dv, rope_done)
        for f in range(8):
            c.w(c.pe, "pe", acte)
            c.i(c.pe.transpose(ps_t, qr[:, f * 128:(f + 1) * 128], ident),
                "pe", pes, 1)
            c.w(c.act, "act", pes)
            c.i(c.act.copy(QT[:, f, ts_], ps_t), "act", acte, 1)
        c.w(c.pe, "pe", acte)
        c.i(c.pe.transpose(ps_t[0:64, :], kr, ident), "pe", pes, 1)
        c.w(c.act, "act", pes)
        c.i(c.act.copy(KT2[0:64, ts_], ps_t[0:64, 0:128]), "act", acte, 1)
        c.i(c.act.copy(KT2[64:128, ts_], ps_t[0:64, 0:128]), "act", acte, 1)
    qkv_done = c.sems["acte"].n
    stkA.close()
    cst2 = c.sem("cst_l1a2")
    wob = sb("l1a_wob", [128, 8, 1024], BF16)
    c.w(c.pool, "pool", acte, qkv_done)
    c.w(c.pool, "pool", pes, c.sems["pe"].n)
    c.w(c.pool, "pool", dv, c.sems["dv"].n)
    c.i(c.pool.dma_start(out=wob, in_=io["l1a_wo"].rearrange("kc p o -> p kc o")),
        "pool", cst2, 16)
    pb = [sb(f"l1a_pb{i}", [128, 512], BF16) for i in range(2)]
    opb = [sb(f"l1a_opb{i}", [128, 8, 512], BF16) for i in range(2)]
    sbo = [sb(f"l1a_sbo{i}", [64, 512], BF16) for i in range(2)]
    cm = sb("l1a_cm", [128, 1], F32)
    tcm = sb("l1a_tcm", [128, 1], F32)
    negc1 = sb("l1a_negc1", [1, 1], F32)
    negc = sb("l1a_negc", [128, 1], F32)
    rd = sb("l1a_rd", [1, 512], F32)
    sbb = sb("l1a_sbb", [64, 512], F32)

    env = {"ps_s": ps_s, "ps_o": ps_o, "ps_b": ps_b, "ps_d": ps_d,
           "pb": pb, "sbo": sbo, "stp": stp, "cm": cm, "tcm": tcm,
           "negc1": negc1, "negc": negc, "rd": rd, "sbb": sbb,
           "op_dram": op_dram}
    for h in range(16):
        base = (h % 2) * 64
        qrows = slice(base, base + 64)
        for qb in range(8):
            nkc = 4 * (qb + 1)
            qs = slice(qb * 512, (qb + 1) * 512)

            def mask_fn(kc, _nkc=nkc):
                if kc >= _nkc - 4:
                    return dmk[:, kc - (_nkc - 4), :]
                return None

            emit_attn_hqb(
                c, env,
                lambda kc, _b=base: KT2[_b:_b + 64, kc * 128:(kc + 1) * 128],
                QT[qrows, h // 2, qs],
                lambda kc: vt[:, kc, 0:65],
                nkc, mask_fn, 0.125,
                slice(h * 64, (h + 1) * 64), qs, (h * 8 + qb) % 2, qkv_done)

    emit_outproj_residual(
        c, env,
        lambda kc, db: wob[:, kc, db * 512:(db + 1) * 512],
        None, xin, xout, opb, opl, 8, False, extra_pe_sems=[cst2])
    stk.close()






def prep_l2_ffn(inputs):
    Wg = np.asarray(inputs["L2_ffn_W_gate"], dtype=np.float32)
    bg = np.asarray(inputs["L2_ffn_b_gate"], dtype=np.float32)
    W1 = np.asarray(inputs["L2_ffn_W1"], dtype=np.float32)
    b1 = np.asarray(inputs["L2_ffn_b1"], dtype=np.float32)
    W2 = np.asarray(inputs["L2_ffn_W2"], dtype=np.float32)
    b2 = np.asarray(inputs["L2_ffn_b2"], dtype=np.float32)
    w1t = np.ascontiguousarray(
        W1.reshape(8, 1024, 8, 128).transpose(0, 2, 1, 3))
    w2t = np.ascontiguousarray(W2.reshape(8, 8, 128, 1024))
    b1t = np.stack([scales_tile(b1[e]) for e in range(8)])
    b2t = np.stack([bcast_pp(b2[e]) for e in range(8)])
    return {
        "l2f_wg": np.ascontiguousarray(Wg.reshape(8, 128, 8)),
        "l2f_bg": bcast_pp(bg),
        "l2f_w1": w1t, "l2f_b1": b1t, "l2f_w2": w2t, "l2f_b2": b2t,
    }


def decl_l2_ffn(nc):
    io = {}
    io["l2f_wg"] = nc.dram_tensor("l2f_wg", [8, 128, 8], F32, kind="ExternalInput").ap()
    io["l2f_bg"] = nc.dram_tensor("l2f_bg", [128, 8], F32, kind="ExternalInput").ap()
    io["l2f_w1"] = nc.dram_tensor("l2f_w1", [8, 8, 1024, 128], F32, kind="ExternalInput").ap()
    io["l2f_b1"] = nc.dram_tensor("l2f_b1", [8, 128, 8], F32, kind="ExternalInput").ap()
    io["l2f_w2"] = nc.dram_tensor("l2f_w2", [8, 8, 128, 1024], F32, kind="ExternalInput").ap()
    io["l2f_b2"] = nc.dram_tensor("l2f_b2", [8, 128, 1024], F32, kind="ExternalInput").ap()
    return io


def emit_l2_ffn(c, xin, xout, io):
    nc = c.nc
    stk = ExitStack()
    sb = lambda n, shp, dt: stk.enter_context(nc.sbuf_tensor(n, shp, dt)).ap()
    cst = c.sem("cst_l2f")
    dv = c.sem("dv")
    pes = c.sem("pe")
    acte = c.sem("acte")
    sx = c.sem("sx")
    wsA = [c.sem("wA0"), c.sem("wA1")]
    wsB = [c.sem("wB0"), c.sem("wB1")]
    T2 = 2048
    ln_fm = sb("l2f_ln", [128, 8, T2], BF16)
    h_fm = sb("l2f_h", [128, 8, T2], BF16)
    acc = sb("l2f_acc", [128, 16, 1024], BF16)
    coef = sb("l2f_coef", [128, 16, 8], F32)
    top8 = sb("l2f_top8", [128, 8], F32)
    gmask = sb("l2f_gmask", [128, 8], F32)
    wgb = sb("l2f_wgb", [128, 8, 8], BF16)
    bgb = sb("l2f_bgb", [128, 8], F32)
    b1b = [sb(f"l2f_b1b{i}", [128, 8], F32) for i in range(2)]
    b2b = [sb(f"l2f_b2b{i}", [128, 1024], F32) for i in range(2)]
    w1b = [sb(f"l2f_w1b{i}", [128, 8, 128], BF16) for i in range(2)]
    w2b = [sb(f"l2f_w2b{i}", [128, 8, 1024], BF16) for i in range(2)]
    ytmp = sb("l2f_ytmp", [128, 1024], F32)
    c.sb["ps_l2f"] = c.ps8[:, 0:4, :].rearrange("p (a b) f -> p a b f", b=2)
    ps_g = c.psbank(4)
    ps_y = [c.psbank(4), c.psbank(5)]
    sl = lambda buf: (lambda kc, tb: buf[:, kc, tb * 512:(tb + 1) * 512])

    c.i(c.pool.dma_start(out=wgb, in_=io["l2f_wg"].rearrange("kc p o -> p kc o")),
        "pool", cst, 16)
    c.i(c.pool.dma_start(out=bgb, in_=io["l2f_bg"]), "pool", cst, 16)

    for half in range(2):
        toff = half * 16
        for q in range(4):
            emit_ln_fm_quarter(c, xin, half * 4 + q,
                               ln_fm[:, :, q * 512:(q + 1) * 512],
                               c.bufs, ntiles=4)
        ln_done = c.sems["acte"].n
        c.w(c.pe, "pe", acte, ln_done)
        c.w(c.pe, "pe", cst)
        for i in range(16):
            c.w(c.pe, "pe", dv, c.sems["dv"].n)
            last = None
            for kc in range(8):
                last = c.pe.matmul(ps_g[:, 0:8], ln_fm[:, kc, i * 128:(i + 1) * 128].rearrange("p t -> t p")
                                   if False else ln_fm[:, kc, i * 128:(i + 1) * 128],
                                   wgb[:, kc, :], start=(kc == 0), stop=(kc == 7))
            c.i(last, "pe", pes, 1)
            c.w(c.dve, "dve", pes)
            c.dop(c.dve.tensor_tensor(coef[:, i, :], ps_g[:, 0:8], bgb,
                                      ALU.add))
            c.dop(c.dve.max(out=top8, in_=coef[:, i, :]))
            c.dop(c.dve.tensor_single_scalar(gmask, coef[:, i, :],
                                             top8[:, 1:2], ALU.is_ge))
            c.i(c.dve.tensor_tensor(coef[:, i, :], coef[:, i, :],
                                    gmask, ALU.mult), "dve", dv, 1)
        gate_done = c.sems["dv"].n
        c.w(c.dve, "dve", sx)
        c.i(c.dve.memset(acc, 0.0), "dve", dv, 1)
        for e in range(8):
            c.w(c.pool, "pool", acte, c.sems["acte"].n)
            c.w(c.pool, "pool", dv, c.sems["dv"].n)
            c.i(c.pool.dma_start(out=b1b[e % 2], in_=io["l2f_b1"][e]), "pool",
                c.sem(f"cst_l2f_b{e % 2}"), 16)
            c.i(c.pool.dma_start(out=b2b[e % 2], in_=io["l2f_b2"][e]), "pool",
                c.sem(f"cst_l2f_b{e % 2}"), 16)
            for sub in range(2):
                so = sub * 1024
                emit_linear_fm(
                    c, "ps_l2f", 1024, 1024, 1024, io["l2f_w1"][e],
                    (lambda kc, tb, _so=so:
                     ln_fm[:, kc, _so + tb * 512:_so + (tb + 1) * 512]),
                    (lambda og, tb, _so=so:
                     h_fm[:, og, _so + tb * 512:_so + (tb + 1) * 512]),
                    scale_ap_fn=None, act_fn=AF.Relu, wbufs=w1b,
                    wsems=wsA, bias_ap_fn=lambda og, _e=e: b1b[_e % 2][:, og:og + 1],
                    extra_pe_waits=[(c.pe, "pe",
                                     c.sem(f"cst_l2f_b{e % 2}"), None)])
            c.w(c.pool, "pool", pes, c.sems["pe"].n)
            c.i(c.pool.dma_start(out=w2b[e % 2],
                                 in_=io["l2f_w2"][e].rearrange("kc p o -> p kc o")),
                "pool", wsB[e % 2], 16)
            c.w(c.pe, "pe", wsB[e % 2])
            c.w(c.pe, "pe", acte)
            for i in range(16):
                for db in range(2):
                    c.w(c.pe, "pe", dv, c.sems["dv"].n)
                    last = None
                    for kc in range(8):
                        last = c.pe.matmul(
                            ps_y[db], h_fm[:, kc, i * 128:(i + 1) * 128],
                            w2b[e % 2][:, kc, db * 512:(db + 1) * 512],
                            start=(kc == 0), stop=(kc == 7))
                    c.i(last, "pe", pes, 1)
                mm_done = c.sems["pe"].n
                c.w(c.dve, "dve", pes, mm_done)
                if i == 0:
                    c.w(c.dve, "dve", c.sem(f"cst_l2f_b{e % 2}"))
                cc = coef[:, i, e:e + 1]
                for db in range(2):
                    dsl = slice(db * 512, (db + 1) * 512)
                    c.dop(c.dve.tensor_tensor(ytmp[:, dsl], ps_y[db],
                                              b2b[e % 2][:, dsl], ALU.add))
                    c.i(c.dve.scalar_tensor_tensor(
                        acc[:, toff + i - toff, dsl] if False else acc[:, i, dsl],
                        ytmp[:, dsl], cc, acc[:, i, dsl], ALU.mult, ALU.add),
                        "dve", dv, 1)
                    c.w(c.dve, "dve", dv)
        for i in range(16):
            tt = toff + i
            xo = c.bufs["xo"][i % 2]
            xsem = c.sem(f"xo{i % 2}")
            c.w(c.sp, "sp", sx)
            c.w(c.sp, "sp", dv, c.sems["dv"].n)
            c.i(c.sp.dma_start(out=xo, in_=xin[tt * 128:(tt + 1) * 128, :]),
                "sp", xsem, 16)
            c.w(c.dve, "dve", xsem)
            c.i(c.dve.tensor_tensor(xo, xo, acc[:, i, :], ALU.add), "dve", dv, 1)
            c.w(c.pool, "pool", dv)
            c.i(c.pool.dma_start(out=xout[tt * 128:(tt + 1) * 128, :], in_=xo),
                "pool", sx, 16)
    stk.close()






def prep_l3_ffn(inputs):
    rw = np.asarray(inputs["L3_ffn_router_w"], dtype=np.float32)
    rb = np.asarray(inputs["L3_ffn_router_b"], dtype=np.float32)
    w1 = np.asarray(inputs["L3_ffn_w1"], dtype=np.float32)
    w2 = np.asarray(inputs["L3_ffn_w2"], dtype=np.float32)
    w3 = np.asarray(inputs["L3_ffn_w3"], dtype=np.float32)
    g = np.asarray(inputs["L3_ffn_ln_g"], dtype=np.float32)
    b = np.asarray(inputs["L3_ffn_ln_b"], dtype=np.float32)
    w1t = np.stack([wT_ogtiles(w1[e].T) for e in range(16)])
    w3t = np.stack([wT_ogtiles(w3[e].T) for e in range(16)])
    w2t = np.stack([np.ascontiguousarray(w2[e].T.reshape(16, 128, 1024))
                    for e in range(16)])
    gt = np.stack([bcast_pp(g[e]) for e in range(16)])
    bt = np.stack([bcast_pp(b[e]) for e in range(16)])
    return {
        "l3f_rw": np.ascontiguousarray(rw.T.reshape(8, 128, 16)),
        "l3f_rb": bcast_pp(rb),
        "l3f_w1": w1t, "l3f_w3": w3t, "l3f_w2": w2t,
        "l3f_g": gt, "l3f_b": bt,
    }


def decl_l3_ffn(nc):
    io = {}
    io["l3f_rw"] = nc.dram_tensor("l3f_rw", [8, 128, 16], F32, kind="ExternalInput").ap()
    io["l3f_rb"] = nc.dram_tensor("l3f_rb", [128, 16], F32, kind="ExternalInput").ap()
    io["l3f_w1"] = nc.dram_tensor("l3f_w1", [16, 16, 1024, 128], F32, kind="ExternalInput").ap()
    io["l3f_w3"] = nc.dram_tensor("l3f_w3", [16, 16, 1024, 128], F32, kind="ExternalInput").ap()
    io["l3f_w2"] = nc.dram_tensor("l3f_w2", [16, 16, 128, 1024], F32, kind="ExternalInput").ap()
    io["l3f_g"] = nc.dram_tensor("l3f_g", [16, 128, 1024], F32, kind="ExternalInput").ap()
    io["l3f_b"] = nc.dram_tensor("l3f_b", [16, 128, 1024], F32, kind="ExternalInput").ap()
    return io


def emit_l3_ffn(c, xin, xout, io):
    nc = c.nc
    stk = ExitStack()
    sb = lambda n, shp, dt: stk.enter_context(nc.sbuf_tensor(n, shp, dt)).ap()
    cst = c.sem("cst_l3f")
    dv = c.sem("dv")
    pes = c.sem("pe")
    acte = c.sem("acte")
    sx = c.sem("sx")
    wsA = [c.sem("wA0"), c.sem("wA1")]
    wsB = [c.sem("wB0"), c.sem("wB1")]
    wsC = [c.sem("wC0"), c.sem("wC1")]
    T2 = 1024
    ln_fm = sb("l3f_ln", [128, 8, T2], BF16)
    h_fm = sb("l3f_h", [128, 16, T2], BF16)
    acc = sb("l3f_acc", [128, 8, 1024], BF16)
    coef = sb("l3f_coef", [128, 8, 16], F32)
    top8 = sb("l3f_top8", [128, 8], F32)
    gmask = sb("l3f_gmask", [128, 16], F32)
    probs = sb("l3f_probs", [128, 16], F32)
    rstat = sb("l3f_rstat", [128, 8], F32)
    rwb = sb("l3f_rwb", [128, 8, 16], BF16)
    rbb = sb("l3f_rbb", [128, 16], F32)
    w1b = [sb(f"l3f_w1b{i}", [128, 8, 128], BF16) for i in range(2)]
    w3b = [sb(f"l3f_w3b{i}", [128, 8, 128], BF16) for i in range(2)]
    w2b = sb("l3f_w2b", [128, 16, 1024], BF16)
    gb = [sb(f"l3f_gb{i}", [128, 1024], F32) for i in range(2)]
    bb = [sb(f"l3f_bb{i}", [128, 1024], F32) for i in range(2)]
    tmpA = [sb(f"l3f_tA{i}", [128, 1024], BF16) for i in range(2)]
    tmpB = [sb(f"l3f_tB{i}", [128, 1024], BF16) for i in range(2)]
    ytmp = sb("l3f_ytmp", [128, 1024], F32)
    c.sb["ps_l3f"] = c.ps8[:, 0:4, :].rearrange("p (a b) f -> p a b f", b=2)
    ps_g = c.psbank(4)
    ps_y = [c.psbank(4), c.psbank(5)]

    c.i(c.pool.dma_start(out=rwb, in_=io["l3f_rw"].rearrange("kc p o -> p kc o")),
        "pool", cst, 16)
    c.i(c.pool.dma_start(out=rbb, in_=io["l3f_rb"]), "pool", cst, 16)

    s_sum, s_neg, s_rcp, s_s2 = (rstat[:, 0:1], rstat[:, 1:2], rstat[:, 2:3],
                                 rstat[:, 3:4])

    for half in range(4):
        toff = half * 8
        for q in range(2):
            emit_ln_fm_quarter(c, xin, half * 2 + q,
                               ln_fm[:, :, q * 512:(q + 1) * 512],
                               c.bufs, ntiles=4)
        ln_done = c.sems["acte"].n
        c.w(c.pe, "pe", acte, ln_done)
        c.w(c.pe, "pe", cst)
        for i in range(8):
            c.w(c.pe, "pe", dv, c.sems["dv"].n)
            last = None
            for kc in range(8):
                last = c.pe.matmul(ps_g[:, 0:16],
                                   ln_fm[:, kc, i * 128:(i + 1) * 128],
                                   rwb[:, kc, :], start=(kc == 0), stop=(kc == 7))
            c.i(last, "pe", pes, 1)
            c.w(c.dve, "dve", pes)
            c.dop(c.dve.tensor_tensor(probs, ps_g[:, 0:16], rbb, ALU.add))
            c.dop(c.dve.tensor_reduce(s_neg, probs, axis=AX, op=ALU.max,
                                      negate=True))
            c.w(c.act, "act", dv)
            c.aop(c.act.activation(probs, probs, AF.Exp, bias=s_neg,
                                   accum_out=s_sum))
            c.w(c.dve, "dve", acte)
            c.dop(c.dve.reciprocal(s_rcp, s_sum))
            c.dop(c.dve.tensor_single_scalar(probs, probs, s_rcp, ALU.mult))
            c.dop(c.dve.max(out=top8, in_=probs))
            c.dop(c.dve.tensor_single_scalar(gmask, probs, top8[:, 1:2],
                                             ALU.is_ge))
            c.dop(c.dve.tensor_tensor(probs, probs, gmask, ALU.mult))
            c.dop(c.dve.tensor_tensor(s_s2, top8[:, 0:1], top8[:, 1:2], ALU.add))
            c.dop(c.dve.reciprocal(s_s2, s_s2))
            c.i(c.dve.tensor_single_scalar(coef[:, i, :], probs, s_s2, ALU.mult),
                "dve", dv, 1)
        c.w(c.dve, "dve", sx)
        c.i(c.dve.memset(acc, 0.0), "dve", dv, 1)
        for e in range(16):
            gsem = c.sem(f"cst_l3f_g{e % 2}")
            c.w(c.pool, "pool", dv, c.sems["dv"].n)
            c.i(c.pool.dma_start(out=gb[e % 2], in_=io["l3f_g"][e]), "pool",
                gsem, 16)
            c.i(c.pool.dma_start(out=bb[e % 2], in_=io["l3f_b"][e]), "pool",
                gsem, 16)
            emit_ab_fused(
                c, "ps_l3f", 1024, 2048, 1024,
                io["l3f_w1"][e], io["l3f_w3"][e],
                (lambda kc, tb: ln_fm[:, kc, tb * 512:(tb + 1) * 512]),
                (lambda og: h_fm[:, og, :]),
                lambda og: 1.0, lambda og: 1.0,
                AF.Silu, w1b, w3b, wsA, wsB, tmpA, tmpB)
            c.w(c.pool, "pool", pes, c.sems["pe"].n)
            c.i(c.pool.dma_start(out=w2b,
                                 in_=io["l3f_w2"][e].rearrange("kc p o -> p kc o")),
                "pool", wsC[e % 2], 16)
            c.w(c.pe, "pe", wsC[e % 2])
            c.w(c.pe, "pe", acte)
            for i in range(8):
                for db in range(2):
                    c.w(c.pe, "pe", dv, c.sems["dv"].n)
                    last = None
                    for kc in range(16):
                        last = c.pe.matmul(
                            ps_y[db], h_fm[:, kc, i * 128:(i + 1) * 128],
                            w2b[:, kc, db * 512:(db + 1) * 512],
                            start=(kc == 0), stop=(kc == 15))
                    c.i(last, "pe", pes, 1)
                mm_done = c.sems["pe"].n
                c.w(c.act, "act", pes, mm_done)
                c.aop(c.act.copy(ytmp[:, 0:512], ps_y[0]))
                c.aop(c.act.copy(ytmp[:, 512:1024], ps_y[1]))
                ev = c.sems["acte"].n
                c.w(c.dve, "dve", acte, ev)
                if i == 0:
                    c.w(c.dve, "dve", gsem)
                st_ = c.bufs["stat"]
                m_, m2_, bias_, std_, rstd_ = (st_[:, 8:9], st_[:, 9:10],
                                               st_[:, 10:11], st_[:, 11:12],
                                               st_[:, 12:13])
                sum_, sq_ = st_[:, 13:14], st_[:, 14:15]
                c.dop(c.dve.reduce_sum(sum_, ytmp, axis=AX))
                c.dop(c.dve.tensor_scalar_mul(m_, sum_, 1.0 / D))
                c.dop(c.dve.tensor_tensor(m2_, m_, m_, ALU.mult))
                c.i(c.dve.tensor_scalar(bias_, m2_, -1.0, 1e-5, ALU.mult,
                                        ALU.add), "dve", dv, 1)
                c.w(c.act, "act", dv)
                c.aop(c.act.activation(c.bufs["sq"], ytmp, AF.Square,
                                       accum_out=sq_))
                c.i(c.act.activation(std_, sq_, AF.Sqrt, bias=bias_,
                                     scale=1.0 / D), "act", acte, 1)
                c.w(c.dve, "dve", acte)
                c.dop(c.dve.reciprocal(rstd_, std_))
                c.dop(c.dve.scalar_tensor_tensor(
                    ytmp, ytmp, m_, rstd_.to_broadcast([128, D]),
                    ALU.subtract, ALU.mult))
                cc = coef[:, i, e:e + 1]
                c.dop(c.dve.tensor_tensor(ytmp, ytmp, gb[e % 2], ALU.mult))
                c.dop(c.dve.scalar_tensor_tensor(
                    acc[:, i, :], ytmp, cc, acc[:, i, :], ALU.mult, ALU.add))
                c.dop(c.dve.scalar_tensor_tensor(
                    acc[:, i, :], bb[e % 2], cc, acc[:, i, :], ALU.mult,
                    ALU.add))
        for i in range(8):
            tt = toff + i
            xo = c.bufs["xo"][i % 2]
            xsem = c.sem(f"xo{i % 2}")
            c.w(c.sp, "sp", sx)
            c.w(c.sp, "sp", dv, c.sems["dv"].n)
            c.i(c.sp.dma_start(out=xo, in_=xin[tt * 128:(tt + 1) * 128, :]),
                "sp", xsem, 16)
            c.w(c.dve, "dve", xsem)
            c.i(c.dve.tensor_tensor(xo, xo, acc[:, i, :], ALU.add), "dve", dv, 1)
            c.w(c.pool, "pool", dv)
            c.i(c.pool.dma_start(out=xout[tt * 128:(tt + 1) * 128, :], in_=xo),
                "pool", sx, 16)
    stk.close()






def prep_l2_attn(inputs):
    g = lambda n: np.asarray(inputs["L2_attn_" + n], dtype=np.float32)
    Wi, bi = g("Wi"), g("bi")
    cw, cb = g("conv_w"), g("conv_b")
    Wp, bp = g("Wp"), g("bp")
    Wh, bh = g("Wh"), g("bh")
    Wq, bq = g("Wq"), g("bq")
    Wk, bk = g("Wk"), g("bk")
    Wv, bv = g("Wv"), g("bv")
    W1, b1 = g("W1"), g("b1")
    W2, b2 = g("W2"), g("b2")
    Wo, bo = g("Wo"), g("bo")
    r = np.arange(128)[:, None]
    j = np.arange(192)[None, :]
    band = (j >= r) & (j <= r + 63)
    bm_main = np.where(band, 0.0, -1e30).astype(np.float32)
    bm_first = np.where(band & (j >= 64), 0.0, -1e30).astype(np.float32)
    cwt = np.ascontiguousarray(cw.T.reshape(2, 128, 5).transpose(1, 0, 2))
    return {
        "l2a_wi": wT_ogtiles(Wi), "l2a_bi": scales_tile(bi),
        "l2a_cw": cwt, "l2a_cb": scales_tile(cb),
        "l2a_wp": wT_ogtiles(Wp), "l2a_bp": scales_tile(bp),
        "l2a_wh2": np.ascontiguousarray(Wh[:, 2].reshape(2, 128).T),
        "l2a_bh2": np.ascontiguousarray(bh[2:3].reshape(1, 1)),
        "l2a_wq": wT_ogtiles(Wq), "l2a_bq": scales_tile(bq),
        "l2a_wk": wT_ogtiles(Wk), "l2a_bk": scales_tile(bk),
        "l2a_wv": wT_ogtiles(Wv), "l2a_bv": scales_tile(bv),
        "l2a_w1": wT_ogtiles(W1), "l2a_b1": scales_tile(b1),
        "l2a_w2": wT_ogtiles(W2), "l2a_b2": scales_tile(b2),
        "l2a_wo": np.ascontiguousarray(Wo.reshape(2, 128, 1024)),
        "l2a_bo": bcast_pp(bo),
        "l2a_bm": np.stack([bm_main, bm_first]),
    }


def decl_l2_attn(nc):
    io = {}
    dt_ = lambda n, shp: nc.dram_tensor(n, shp, F32, kind="ExternalInput").ap()
    io["l2a_wi"] = dt_("l2a_wi", [2, 1024, 128])
    io["l2a_bi"] = dt_("l2a_bi", [128, 2])
    io["l2a_cw"] = dt_("l2a_cw", [128, 2, 5])
    io["l2a_cb"] = dt_("l2a_cb", [128, 2])
    io["l2a_wp"] = dt_("l2a_wp", [2, 256, 128])
    io["l2a_bp"] = dt_("l2a_bp", [128, 2])
    io["l2a_wh2"] = dt_("l2a_wh2", [128, 2])
    io["l2a_bh2"] = dt_("l2a_bh2", [1, 1])
    for n in ["wq", "wk", "wv", "wp2_unused"]:
        pass
    io["l2a_wq"] = dt_("l2a_wq", [2, 256, 128])
    io["l2a_bq"] = dt_("l2a_bq", [128, 2])
    io["l2a_wk"] = dt_("l2a_wk", [2, 256, 128])
    io["l2a_bk"] = dt_("l2a_bk", [128, 2])
    io["l2a_wv"] = dt_("l2a_wv", [2, 256, 128])
    io["l2a_bv"] = dt_("l2a_bv", [128, 2])
    io["l2a_w1"] = dt_("l2a_w1", [2, 1024, 128])
    io["l2a_b1"] = dt_("l2a_b1", [128, 2])
    io["l2a_w2"] = dt_("l2a_w2", [2, 256, 128])
    io["l2a_b2"] = dt_("l2a_b2", [128, 2])
    io["l2a_wo"] = dt_("l2a_wo", [2, 128, 1024])
    io["l2a_bo"] = dt_("l2a_bo", [128, 1024])
    io["l2a_bm"] = dt_("l2a_bm", [2, 128, 192])
    return io


def emit_small_linear(c, u, wb, bias, in_fn, out_fn, nkc, nog, ntb, act_fn,
                      ps, tbw=512):
    dv = c.sem("dv")
    pes = c.sem("pe")
    acte = c.sem("acte")
    for og in range(nog):
        for tb in range(ntb):
            c.w(c.pe, "pe", acte)
            last = None
            for kc in range(nkc):
                last = c.pe.matmul(ps[:, 0:tbw], wb[:, kc, og * 128:(og + 1) * 128],
                                   in_fn(kc, tb), start=(kc == 0),
                                   stop=(kc == nkc - 1))
            c.i(last, "pe", pes, 1)
            c.w(c.act, "act", pes)
            fn = act_fn
            if bias is not None and fn == AF.Copy:
                fn = AF.Identity
            c.i(c.act.activation(out_fn(og, tb), ps[:, 0:tbw], fn,
                                 bias=bias(og) if bias is not None else 0.0),
                "act", acte, 1)


def emit_l2_attn(c, xin, xout, io):
    nc = c.nc
    stk = ExitStack()
    sb = lambda n, shp, dt: stk.enter_context(nc.sbuf_tensor(n, shp, dt)).ap()
    cst = c.sem("cst_l2a")
    dv = c.sem("dv")
    pes = c.sem("pe")
    acte = c.sem("acte")
    sx = c.sem("sx")
    ident = c.consts["ident_bf"]
    ones_row = c.consts["ones_row"]

    PAD = 64
    h_fm = sb("l2a_h", [128, 2, T + 4], BF16)
    top8 = sb("l2a_top8", [128, 8], F32)
    negm = sb("l2a_negm", [128, 1], F32)
    msk = sb("l2a_msk", [128, 192], F32)
    et = sb("l2a_et", [128, 192], F32)
    den = sb("l2a_den", [128, 1], F32)
    wt = sb("l2a_wt", [128, 192], BF16)
    bm = sb("l2a_bm_sb", [128, 2, 192], F32)
    wib = sb("l2a_wib", [128, 8, 256], BF16)
    wqb = sb("l2a_wqb", [128, 2, 256], BF16)
    wkb = sb("l2a_wkb", [128, 2, 256], BF16)
    wvb = sb("l2a_wvb", [128, 2, 256], BF16)
    wpb = sb("l2a_wpb", [128, 2, 256], BF16)
    w1b = sb("l2a_w1b", [128, 8, 256], BF16)
    w2b = sb("l2a_w2b", [128, 2, 256], BF16)
    whb = sb("l2a_whb", [128, 2], BF16)
    bh2 = sb("l2a_bh2_sb", [1, 1], F32)
    wob = sb("l2a_wob", [128, 2, 1024], BF16)
    bob = sb("l2a_bob", [128, 1024], F32)
    cwb = sb("l2a_cwb", [128, 2, 5], F32)
    biases = {}
    for n in ["bi", "cb", "bp", "bq", "bk", "bv", "b1", "b2"]:
        biases[n] = sb(f"l2a_{n}_sb", [128, 2], F32)
        c.i(c.pool.dma_start(out=biases[n], in_=io[f"l2a_{n[0] == 'c' and 'cb' or n}"]
                             if False else io[f"l2a_{n}"]), "pool", cst, 16)
    for nm, buf, key, kcn in [("wi", wib, "l2a_wi", 8), ("wq", wqb, "l2a_wq", 2),
                              ("wk", wkb, "l2a_wk", 2), ("wv", wvb, "l2a_wv", 2),
                              ("wp", wpb, "l2a_wp", 2), ("w1", w1b, "l2a_w1", 8),
                              ("w2", w2b, "l2a_w2", 2)]:
        for og in range(2):
            c.i(c.pool.dma_start(
                out=buf[:, :, og * 128:(og + 1) * 128],
                in_=io[key][og].rearrange("(kc p) o -> p kc o", p=128)),
                "pool", cst, 16)
    c.i(c.pool.dma_start(out=whb, in_=io["l2a_wh2"]), "pool", cst, 16)
    c.i(c.pool.dma_start(out=bh2, in_=io["l2a_bh2"]), "pool", cst, 16)
    c.i(c.pool.dma_start(out=wob, in_=io["l2a_wo"].rearrange("kc p o -> p kc o")),
        "pool", cst, 16)
    c.i(c.pool.dma_start(out=bob, in_=io["l2a_bo"]), "pool", cst, 16)
    c.i(c.pool.dma_start(out=cwb, in_=io["l2a_cw"]), "pool", cst, 16)
    c.i(c.pool.dma_start(out=bm, in_=io["l2a_bm"].rearrange("m p j -> p m j")),
        "pool", cst, 16)

    stkA = ExitStack()
    ln_fm = stkA.enter_context(nc.sbuf_tensor("l2a_ln", [128, 8, T], BF16)).ap()
    for q in range(8):
        emit_ln_fm_quarter(c, xin, q, ln_fm, c.bufs, ntiles=4, global_dst=True)
    ln_done = c.sems["acte"].n

    ps_m = c.psbank(0)
    ps_m2 = c.psbank(1)
    ps_b = c.psbank(3)
    ps_o = c.psbank(2)
    ps_d = [c.psbank(4), c.psbank(5)]

    c.w(c.pe, "pe", acte, ln_done)
    c.w(c.pe, "pe", cst)
    emit_small_linear(c, "hin", wib, lambda og: biases["bi"][:, og:og + 1],
                      lambda kc, tb: ln_fm[:, kc, tb * 512:(tb + 1) * 512],
                      lambda og, tb: h_fm[:, og, 4 + tb * 512:4 + (tb + 1) * 512],
                      8, 2, 8, AF.Copy, ps_m)
    stkA.close()
    mx_fm = sb("l2a_mx", [128, 2, T], BF16)
    phb32 = sb("l2a_ph32", [128, 2, 256], F32)
    phb = sb("l2a_ph", [128, 2, 256], BF16)
    pbb = sb("l2a_pb", [128, 2, 256], BF16)
    pbf = sb("l2a_pbf", [128, 2, T], BF16)
    q_fm = sb("l2a_q", [128, 2, T], BF16)
    k_fm = sb("l2a_k", [128, 2, T + PAD], BF16)
    v_fm = sb("l2a_v", [128, 2, T + PAD], BF16)
    sm_fm = sb("l2a_sm", [128, 2, T], BF16)
    d1_fm = q_fm
    gate = sb("l2a_gate", [1, 512], F32)
    vsh = sb("l2a_vsh", [128, 32, 256], BF16)
    vtm = sb("l2a_vtm", [128, 32, 256], BF16)
    wtT = sb("l2a_wtT", [128, 2, 128], BF16)
    c.w(c.dve, "dve", acte, c.sems["acte"].n)
    c.w(c.dve, "dve", pes, c.sems["pe"].n)
    c.i(c.dve.memset(h_fm[:, :, 0:4], 0.0), "dve", dv, 1)
    c.i(c.dve.memset(k_fm[:, :, 0:PAD], 0.0), "dve", dv, 1)
    c.i(c.dve.memset(v_fm[:, :, 0:PAD], 0.0), "dve", dv, 1)
    pads_done = c.sems["dv"].n

    for step in range(2):
        c.w(c.dve, "dve", acte, c.sems["acte"].n)
        c.w(c.dve, "dve", pes, c.sems["pe"].n)
        c.w(c.dve, "dve", dv, pads_done)
        for og in range(2):
            hsl = lambda k: h_fm[:, og, k:k + T]
            c.dop(c.dve.tensor_scalar_mul(mx_fm[:, og, :], hsl(0),
                                          cwb[:, og, 0:1]))
            for k in range(1, 5):
                c.dop(c.dve.scalar_tensor_tensor(
                    mx_fm[:, og, :], hsl(k), cwb[:, og, k:k + 1],
                    mx_fm[:, og, :], ALU.mult, ALU.add))
            c.dop(c.dve.tensor_scalar_add(mx_fm[:, og, :], mx_fm[:, og, :],
                                          biases["cb"][:, og:og + 1]))
        for og in range(2):
            c.dop(c.dve.tensor_reduce(
                phb32[:, og, :], h_fm[:, og, 4:4 + T].rearrange(
                    "p (c s) -> p c s", s=16), axis=AX, op=ALU.add))
            c.dop(c.dve.tensor_scalar_mul(phb[:, og, :], phb32[:, og, :],
                                          1.0 / 16))
        conv_done = c.sems["dv"].n
        c.w(c.pe, "pe", dv, conv_done)
        emit_small_linear(c, "pb", wpb, lambda og: biases["bp"][:, og:og + 1],
                          lambda kc, tb: phb[:, kc, :].bitcast(F32) if False else phb[:, kc, :],
                          lambda og, tb: pbb[:, og, :],
                          2, 2, 1, AF.Copy, ps_m, tbw=256)
        c.w(c.dve, "dve", acte)
        for og in range(2):
            c.dop(c.dve.tensor_copy(
                pbf[:, og, :].rearrange("p (c s) -> p c s", s=16),
                pbb[:, og, :].unsqueeze(-1).to_broadcast([128, 256, 16])))
        emit_small_linear(c, "q", wqb, lambda og: biases["bq"][:, og:og + 1],
                          lambda kc, tb: h_fm[:, kc, 4 + tb * 512:4 + (tb + 1) * 512],
                          lambda og, tb: q_fm[:, og, tb * 512:(tb + 1) * 512],
                          2, 2, 8, AF.Copy, ps_m)
        emit_small_linear(c, "k", wkb, lambda og: biases["bk"][:, og:og + 1],
                          lambda kc, tb: h_fm[:, kc, 4 + tb * 512:4 + (tb + 1) * 512],
                          lambda og, tb: k_fm[:, og, PAD + tb * 512:PAD + (tb + 1) * 512],
                          2, 2, 8, AF.Copy, ps_m)
        emit_small_linear(c, "v", wvb, lambda og: biases["bv"][:, og:og + 1],
                          lambda kc, tb: h_fm[:, kc, 4 + tb * 512:4 + (tb + 1) * 512],
                          lambda og, tb: v_fm[:, og, PAD + tb * 512:PAD + (tb + 1) * 512],
                          2, 2, 8, AF.Copy, ps_m)
        qkv_done = c.sems["acte"].n
        c.w(c.pe, "pe", acte, qkv_done)
        for tt in range(32):
            for og in range(2):
                c.w(c.pe, "pe", acte)
                c.i(c.pe.transpose(c.bufs["tp"][0],
                                   v_fm[:, og, tt * 128:tt * 128 + 128], ident),
                    "pe", pes, 1)
                c.i(c.pe.transpose(c.bufs["tp"][1],
                                   v_fm[:, og, PAD + tt * 128:PAD + tt * 128 + 128],
                                   ident), "pe", pes, 1)
                c.w(c.act, "act", pes)
                c.i(c.act.copy(vsh[:, tt, og * 128:(og + 1) * 128],
                               c.bufs["tp"][0]), "act", acte, 1)
                c.i(c.act.copy(vtm[:, tt, og * 128:(og + 1) * 128],
                               c.bufs["tp"][1]), "act", acte, 1)
        vt_done = c.sems["acte"].n
        for tt in range(32):
            t0 = tt * 128
            c.w(c.pe, "pe", acte, vt_done)
            c.w(c.pe, "pe", dv, c.sems["dv"].n)
            last = None
            for kc in range(2):
                last = c.pe.matmul(ps_m[:, 0:192], q_fm[:, kc, t0:t0 + 128],
                                   k_fm[:, kc, t0:t0 + 192],
                                   start=(kc == 0), stop=(kc == 1))
            c.i(last, "pe", pes, 1)
            c.w(c.dve, "dve", pes)
            bmi = bm[:, 1, :] if tt == 0 else bm[:, 0, :]
            c.dop(c.dve.tensor_tensor(ps_m[:, 0:192], ps_m[:, 0:192], bmi,
                                      ALU.add))
            c.dop(c.dve.max(out=top8, in_=ps_m[:, 0:192]))
            c.dop(c.dve.tensor_scalar_mul(negm, top8[:, 0:1], -1.0 / 16))
            c.dop(c.dve.tensor_single_scalar(msk[:, 0:192], ps_m[:, 0:192],
                                             top8[:, 7:8], ALU.is_ge))
            c.w(c.act, "act", dv)
            c.aop(c.act.activation(et[:, 0:192], ps_m[:, 0:192], AF.Exp,
                                   bias=negm, scale=1.0 / 16))
            c.w(c.dve, "dve", acte)
            c.dop(c.dve.tensor_tensor(et[:, 0:192], et[:, 0:192], msk[:, 0:192],
                                      ALU.mult))
            c.dop(c.dve.reduce_sum(den, et[:, 0:192], axis=AX))
            c.dop(c.dve.reciprocal(den, den))
            c.dop(c.dve.tensor_single_scalar(wt[:, 0:192], et[:, 0:192], den,
                                             ALU.mult))
            if tt == 0:
                c.dop(c.dve.memset(wt[0:1, :], 0.0))
            wt_done = c.sems["dv"].n
            c.w(c.pe, "pe", dv, wt_done)
            c.i(c.pe.transpose(c.bufs["tp"][0], wt[:, 0:128], ident), "pe", pes, 1)
            c.i(c.pe.transpose(c.bufs["tp"][1][0:64, :], wt[:, 128:192], ident),
                "pe", pes, 1)
            c.w(c.act, "act", pes)
            c.i(c.act.copy(wtT[:, 0, :], c.bufs["tp"][0]), "act", acte, 1)
            c.i(c.act.copy(wtT[64:128, 1, :], c.bufs["tp"][1][0:64, :]),
                "act", acte, 1)
            wtt_done = c.sems["acte"].n
            c.w(c.pe, "pe", acte, wtt_done)
            c.i(c.pe.matmul(ps_o[:, 0:256], wtT[:, 0, :], vsh[:, tt, :],
                            start=True, stop=False), "pe", pes, 1)
            c.i(c.pe.matmul(ps_o[:, 0:256], wtT[64:128, 1, :],
                            vtm[64:128, tt, :], start=False, stop=True),
                "pe", pes, 1)
            c.w(c.act, "act", pes)
            c.aop(c.act.copy(c.bufs["lt"][0][:, 0:256], ps_o[:, 0:256]))
            c.w(c.pe, "pe", acte)
            for og in range(2):
                c.i(c.pe.transpose(c.bufs["tp"][og],
                                   c.bufs["lt"][0][:, og * 128:(og + 1) * 128],
                                   ident), "pe", pes, 1)
            c.w(c.act, "act", pes)
            for og in range(2):
                c.i(c.act.copy(sm_fm[:, og, t0:t0 + 128], c.bufs["tp"][og]),
                    "act", acte, 1)
        att_done = c.sems["acte"].n
        uis = [h_fm, h_fm, mx_fm, mx_fm, pbf, pbf, sm_fm, sm_fm]

        def ui_in(kc, tb):
            buf = uis[kc]
            off = 4 if buf is h_fm else 0
            return buf[:, kc % 2, off + tb * 512:off + (tb + 1) * 512]
        c.w(c.pe, "pe", acte, att_done)
        emit_small_linear(c, "d1", w1b, lambda og: biases["b1"][:, og:og + 1],
                          ui_in,
                          lambda og, tb: d1_fm[:, og, tb * 512:(tb + 1) * 512],
                          8, 2, 8, AF.Tanh, ps_m)
        emit_small_linear(c, "d2", w2b, lambda og: biases["b2"][:, og:og + 1],
                          lambda kc, tb: d1_fm[:, kc, tb * 512:(tb + 1) * 512],
                          lambda og, tb: mx_fm[:, og, tb * 512:(tb + 1) * 512],
                          2, 2, 8, AF.Copy, ps_m)
        delta_done = c.sems["acte"].n
        for tb in range(8):
            c.w(c.pe, "pe", acte)
            c.w(c.pe, "pe", dv, c.sems["dv"].n)
            last = None
            for kc in range(2):
                last = c.pe.matmul(ps_b[0:1, 0:512],
                                   whb[:, kc:kc + 1],
                                   h_fm[:, kc, 4 + tb * 512:4 + (tb + 1) * 512],
                                   start=(kc == 0), stop=(kc == 1))
            c.i(last, "pe", pes, 1)
            c.w(c.act, "act", pes)
            c.aop(c.act.activation(gate, ps_b[0:1, 0:512], AF.Sigmoid,
                                   bias=bh2))
            gate_done = c.sems["acte"].n
            c.w(c.pe, "pe", acte, gate_done)
            c.i(c.pe.matmul(ps_b[:, 0:512], ones_row, gate,
                            start=True, stop=True), "pe", pes, 1)
            c.w(c.dve, "dve", pes)
            c.w(c.dve, "dve", acte, delta_done)
            for og in range(2):
                c.dop(c.dve.tensor_tensor(
                    et[:, 0:0 + 512] if False else c.bufs["sq"][:, 0:512],
                    mx_fm[:, og, tb * 512:(tb + 1) * 512], ps_b[:, 0:512],
                    ALU.mult))
                c.dop(c.dve.tensor_tensor(
                    h_fm[:, og, 4 + tb * 512:4 + (tb + 1) * 512],
                    h_fm[:, og, 4 + tb * 512:4 + (tb + 1) * 512],
                    c.bufs["sq"][:, 0:512], ALU.add))

    stat = c.bufs["stat"]
    s_sum, s_sq = stat[:, 0:1], stat[:, 1:2]
    s_mean, s_m2 = stat[:, 2:3], stat[:, 3:4]
    s_bias, s_std, s_rstd = stat[:, 4:5], stat[:, 5:6], stat[:, 6:7]
    for tt in range(32):
        t0 = tt * 128
        xo = c.bufs["xo"][tt % 2]
        xt = c.bufs["xt"][tt % 2]
        xsem = c.sem(f"xo{tt % 2}")
        c.w(c.sp, "sp", sx)
        c.w(c.sp, "sp", dv, c.sems["dv"].n)
        c.i(c.sp.dma_start(out=xo, in_=xin[t0:t0 + 128, :]), "sp", xsem, 16)
        for db in range(2):
            c.w(c.pe, "pe", dv, c.sems["dv"].n)
            last = None
            for kc in range(2):
                last = c.pe.matmul(ps_d[db], h_fm[:, kc, 4 + t0:4 + t0 + 128],
                                   wob[:, kc, db * 512:(db + 1) * 512],
                                   start=(kc == 0), stop=(kc == 1))
            c.i(last, "pe", pes, 1)
        mm_done = c.sems["pe"].n
        c.w(c.dve, "dve", xsem)
        c.w(c.dve, "dve", acte, c.sems["acte"].n)
        c.dop(c.dve.reduce_sum(s_sum, xo, axis=AX))
        c.dop(c.dve.tensor_scalar_mul(s_mean, s_sum, 1.0 / D))
        c.dop(c.dve.tensor_tensor(s_m2, s_mean, s_mean, ALU.mult))
        c.i(c.dve.tensor_scalar(s_bias, s_m2, -1.0, 1e-5, ALU.mult, ALU.add),
            "dve", dv, 1)
        c.w(c.act, "act", xsem)
        c.w(c.act, "act", dv)
        c.aop(c.act.activation(c.bufs["sq"], xo, AF.Square, accum_out=s_sq))
        c.i(c.act.activation(s_std, s_sq, AF.Sqrt, bias=s_bias, scale=1.0 / D),
            "act", acte, 1)
        c.w(c.dve, "dve", acte)
        c.dop(c.dve.reciprocal(s_rstd, s_std))
        c.dop(c.dve.scalar_tensor_tensor(
            xt, xo, s_mean, s_rstd.to_broadcast([128, D]),
            ALU.subtract, ALU.mult))
        c.w(c.dve, "dve", pes, mm_done)
        c.dop(c.dve.tensor_tensor(xo, xo, xt, ALU.add))
        c.dop(c.dve.tensor_tensor(xo, xo, bob, ALU.add))
        c.i(c.dve.tensor_tensor(xo[:, 0:512], xo[:, 0:512], ps_d[0], ALU.add),
            "dve", dv, 1)
        c.i(c.dve.tensor_tensor(xo[:, 512:1024], xo[:, 512:1024], ps_d[1],
                                ALU.add), "dve", dv, 1)
        c.w(c.pool, "pool", dv)
        c.i(c.pool.dma_start(out=xout[t0:t0 + 128, :], in_=xo), "pool", sx, 16)
    stk.close()





GAMMA = 0.99


def prep_l3_attn(inputs):
    Wq = np.asarray(inputs["L3_attn_Wq"], dtype=np.float32)
    Wk = np.asarray(inputs["L3_attn_Wk"], dtype=np.float32)
    Wv = np.asarray(inputs["L3_attn_Wv"], dtype=np.float32)
    Wo = np.asarray(inputs["L3_attn_Wo"], dtype=np.float32)
    l0 = np.arange(128)
    d = l0[None, :] - l0[:, None]
    W2s = np.where(d > 0, GAMMA ** np.maximum(d, 0), 0.0).astype(np.float32)
    W2i = np.where(d >= 0, GAMMA ** np.maximum(d, 0), 0.0).astype(np.float32)
    M2 = (d >= 0).astype(np.float32)
    g1 = (GAMMA ** (l0 + 1.0)).astype(np.float32)
    gd = np.stack([np.broadcast_to(g1, (32, 128)),
                   np.broadcast_to(g1 ** 2, (32, 128)),
                   np.broadcast_to(-g1, (32, 128))]).astype(np.float32)
    gcol = (GAMMA ** (127.0 - l0)).astype(np.float32).reshape(128, 1)
    return {
        "l3a_wq": np.ascontiguousarray(Wq.reshape(8, 128, 512)),
        "l3a_wk": np.ascontiguousarray(Wk.reshape(8, 128, 512)),
        "l3a_wv": np.ascontiguousarray(Wv.reshape(8, 128, 512)),
        "l3a_wo": np.ascontiguousarray(Wo.reshape(4, 128, 1024)),
        "l3a_masks": np.stack([W2s, W2i, M2]),
        "l3a_gd": np.ascontiguousarray(gd),
        "l3a_gcol": gcol,
    }


def decl_l3_attn(nc):
    io = {}
    dt_ = lambda n, shp: nc.dram_tensor(n, shp, F32, kind="ExternalInput").ap()
    io["l3a_wq"] = dt_("l3a_wq", [8, 128, 512])
    io["l3a_wk"] = dt_("l3a_wk", [8, 128, 512])
    io["l3a_wv"] = dt_("l3a_wv", [8, 128, 512])
    io["l3a_wo"] = dt_("l3a_wo", [4, 128, 1024])
    io["l3a_masks"] = dt_("l3a_masks", [3, 128, 128])
    io["l3a_gd"] = dt_("l3a_gd", [3, 32, 128])
    io["l3a_gcol"] = dt_("l3a_gcol", [128, 1])
    return io


def emit_l3_attn(c, xin, xout, io):
    nc = c.nc
    stk = ExitStack()
    sb = lambda n, shp, dt: stk.enter_context(nc.sbuf_tensor(n, shp, dt)).ap()
    cst = c.sem("cst_l3a")
    dv = c.sem("dv")
    pes = c.sem("pe")
    acte = c.sem("acte")
    sx = c.sem("sx")
    stp = [c.sem("st0"), c.sem("st1")]
    opl = [c.sem("op0"), c.sem("op1")]
    ident = c.consts["ident_bf"]

    Qtm = sb("l3a_qtm", [128, 32, 512], BF16)
    Ktm = sb("l3a_ktm", [128, 32, 512], BF16)
    Vtm = sb("l3a_vtm", [128, 32, 512], BF16)

    stkA = ExitStack()
    ln_fm = stkA.enter_context(nc.sbuf_tensor("l3a_ln", [128, 8, T], BF16)).ap()
    wqkv = stkA.enter_context(nc.sbuf_tensor("l3a_wqkv", [128, 3, 8, 512],
                                             BF16)).ap()
    for q in range(8):
        emit_ln_fm_quarter(c, xin, q, ln_fm, c.bufs, ntiles=4, global_dst=True)
    for mi, key in enumerate(["l3a_wq", "l3a_wk", "l3a_wv"]):
        c.i(c.pool.dma_start(out=wqkv[:, mi, :, :],
                             in_=io[key].rearrange("kc p o -> p kc o")),
            "pool", cst, 16)
    ps_q = c.psbank(0)
    c.w(c.pe, "pe", cst)
    c.w(c.pe, "pe", acte)
    for tt in range(32):
        for mi, dst in enumerate([Qtm, Ktm, Vtm]):
            c.w(c.pe, "pe", acte)
            last = None
            for kc in range(8):
                last = c.pe.matmul(ps_q, ln_fm[:, kc, tt * 128:(tt + 1) * 128],
                                   wqkv[:, mi, kc, :],
                                   start=(kc == 0), stop=(kc == 7))
            c.i(last, "pe", pes, 1)
            c.w(c.act, "act", pes)
            c.i(c.act.copy(dst[:, tt, :], ps_q), "act", acte, 1)
    qkv_done = c.sems["acte"].n
    c.w(c.pool, "pool", acte, qkv_done)
    c.w(c.pool, "pool", pes, c.sems["pe"].n)
    c.w(c.pool, "pool", dv, c.sems["dv"].n)
    stkA.close()
    cst2 = c.sem("cst_l3a2")
    wob = sb("l3a_wob", [128, 4, 1024], BF16)
    msk = sb("l3a_msk", [128, 3, 128], F32)
    gd = sb("l3a_gd_sb", [32, 3, 128], F32)
    gcol = sb("l3a_gcol_sb", [128, 1], F32)
    qt_c = sb("l3a_qt_c", [32, 128], BF16)
    kt_c = sb("l3a_kt_c", [32, 128], BF16)
    qt1 = sb("l3a_qt1", [32, 128], BF16)
    qt2 = sb("l3a_qt2", [32, 128], BF16)
    qt1n = sb("l3a_qt1n", [32, 128], BF16)
    k1e = sb("l3a_k1e", [128, 32], BF16)
    q1e = sb("l3a_q1e", [128, 32], BF16)
    aw2s = sb("l3a_aw2s", [128, 128], BF16)
    atwi = sb("l3a_atwi", [128, 128], BF16)
    atm = sb("l3a_atm", [128, 128], BF16)
    ptw = sb("l3a_ptw", [128, 128], BF16)
    nw = sb("l3a_nw", [128, 128], BF16)
    utm = sb("l3a_utm", [128, 32], BF16)
    uneg = sb("l3a_uneg", [128, 32], BF16)
    rt = sb("l3a_rt", [32, 128], BF16)
    rmr1 = sb("l3a_rmr1", [32, 128], BF16)
    ytp = sb("l3a_ytp", [32, 128], BF16)
    stil = sb("l3a_stil", [32, 32], BF16)
    stt = sb("l3a_stt", [32, 3, 32], F32)
    stb = sb("l3a_stb", [32, 3, 32], BF16)
    zb = sb("l3a_zb", [32, 32], BF16)
    stage = [sb(f"l3a_stage{i}", [32, T], BF16) for i in range(2)]
    opb = [sb(f"l3a_opb{i}", [128, 8, 512], BF16) for i in range(2)]
    op_dram = nc.dram_tensor("l3a_op", [512, T], BF16, kind="Internal").ap()
    c.i(c.pool.dma_start(out=wob, in_=io["l3a_wo"].rearrange("kc p o -> p kc o")),
        "pool", cst2, 16)
    c.i(c.pool.dma_start(out=msk, in_=io["l3a_masks"].rearrange("m p j -> p m j")),
        "pool", cst2, 16)
    c.i(c.pool.dma_start(out=gd, in_=io["l3a_gd"].rearrange("m p j -> p m j")),
        "pool", cst2, 16)
    c.i(c.pool.dma_start(out=gcol, in_=io["l3a_gcol"]), "pool", cst2, 16)
    mW2s, mW2i, mM2 = msk[:, 0, :], msk[:, 1, :], msk[:, 2, :]
    gd1, gd2, gd1n = gd[:, 0, :], gd[:, 1, :], gd[:, 2, :]
    S0f, C0f, G0f = stt[:, 0, :], stt[:, 1, :], stt[:, 2, :]
    S0b, C0b, G0b = stb[:, 0, :], stb[:, 1, :], stb[:, 2, :]
    c.w(c.dve, "dve", cst2)
    c.w(c.pe, "pe", cst2)

    ps_a = c.psbank(0)
    ps_at = c.psbank(1)
    ps_p = c.psbank(2)
    ps_s3 = c.psbank(3)
    ps_o = c.psbank(4)
    ps_su = c.psbank(5)
    tp = c.bufs["tp"]

    for h in range(16):
        hs = slice(h * 32, (h + 1) * 32)
        hcol = slice((h % 16) * 32, (h % 16) * 32 + 32)
        c.w(c.act, "act", stp[h % 2])
        c.w(c.dve, "dve", pes, c.sems["pe"].n)
        c.w(c.dve, "dve", acte, c.sems["acte"].n)
        c.dop(c.dve.memset(stt, 0.0))
        c.dop(c.dve.memset(stb, 0.0))
        c.dop(c.dve.memset(zb, 0.0))
        for ch in range(32):
            qsl = Qtm[:, ch, hcol]
            ksl = Ktm[:, ch, hcol]
            vsl = Vtm[:, ch, hcol]
            c.w(c.pe, "pe", dv, c.sems["dv"].n)
            c.w(c.pe, "pe", acte, c.sems["acte"].n)
            c.i(c.pe.transpose(tp[0][0:32, :], qsl, ident), "pe", pes, 1)
            c.i(c.pe.transpose(tp[1][0:32, :], ksl, ident), "pe", pes, 1)
            c.w(c.act, "act", pes)
            c.aop(c.act.copy(qt_c, tp[0][0:32, 0:128]))
            c.aop(c.act.copy(kt_c, tp[1][0:32, 0:128]))
            c.w(c.dve, "dve", acte)
            c.dop(c.dve.tensor_tensor(qt1, qt_c, gd1, ALU.mult))
            c.dop(c.dve.tensor_tensor(qt2, qt_c, gd2, ALU.mult))
            c.dop(c.dve.tensor_tensor(qt1n, qt_c, gd1n, ALU.mult))
            c.dop(c.dve.tensor_single_scalar(k1e, ksl, gcol, ALU.mult))
            c.dop(c.dve.tensor_single_scalar(q1e, qsl, gcol, ALU.mult))
            c.w(c.pe, "pe", dv)
            c.i(c.pe.matmul(ps_a[:, 0:128], qt_c, kt_c, start=True, stop=True),
                "pe", pes, 1)
            c.i(c.pe.matmul(ps_at[:, 0:128], kt_c, qt_c, start=True, stop=True),
                "pe", pes, 1)
            c.w(c.dve, "dve", pes)
            c.dop(c.dve.tensor_tensor(aw2s, ps_a[:, 0:128], mW2s, ALU.mult))
            c.dop(c.dve.tensor_tensor(atwi, ps_at[:, 0:128], mW2i, ALU.mult))
            c.dop(c.dve.tensor_tensor(atm, ps_at[:, 0:128], mM2, ALU.mult))
            c.w(c.pe, "pe", dv)
            c.i(c.pe.matmul(ps_s3[:, 0:32], aw2s, vsl, start=True, stop=True),
                "pe", pes, 1)
            c.w(c.act, "act", pes)
            c.aop(c.act.copy(utm, ps_s3[:, 0:32]))
            c.aop(c.act.activation(uneg, ps_s3[:, 0:32], AF.Copy, scale=-1.0))
            c.w(c.pe, "pe", acte)
            c.i(c.pe.matmul(ps_s3[0:32, 32:160], ksl, atwi, start=True,
                            stop=True), "pe", pes, 1)
            c.w(c.act, "act", pes)
            c.aop(c.act.copy(rt, ps_s3[0:32, 32:160]))
            c.w(c.pe, "pe", acte)
            c.i(c.pe.matmul(ps_s3[0:32, 160:288], ksl, atm, start=True,
                            stop=True), "pe", pes, 1)
            c.w(c.dve, "dve", pes)
            c.dop(c.dve.tensor_tensor(rmr1, rt, ps_s3[0:32, 160:288],
                                      ALU.subtract))
            c.dop(c.dve.tensor_tensor(rmr1, rmr1, gd1, ALU.mult))
            c.w(c.pe, "pe", dv)
            c.i(c.pe.matmul(ps_p[:, 0:128], qt_c, rt, start=True, stop=True),
                "pe", pes, 1)
            c.w(c.dve, "dve", pes)
            c.dop(c.dve.tensor_tensor(ptw, ps_p[:, 0:128], mW2i, ALU.mult))
            c.w(c.pe, "pe", dv)
            c.i(c.pe.matmul(ps_s3[0:32, 288:416], S0b, qt1,
                            start=True, stop=True), "pe", pes, 1)
            c.w(c.act, "act", pes)
            c.aop(c.act.copy(ytp, ps_s3[0:32, 288:416]))
            c.w(c.pe, "pe", acte)
            c.i(c.pe.matmul(ps_p[:, 128:256], qt_c, ytp, start=True, stop=True),
                "pe", pes, 1)
            c.w(c.dve, "dve", pes)
            c.dop(c.dve.tensor_tensor(nw, ps_p[:, 128:256], mW2i, ALU.mult))
            c.w(c.pe, "pe", dv)
            c.i(c.pe.matmul(ps_o[0:32, 0:128], vsl, ptw, start=True, stop=False),
                "pe", pes, 1)
            c.pe.matmul(ps_o[0:32, 0:128], uneg, atwi, start=False, stop=False)
            c.pe.matmul(ps_o[0:32, 0:128], vsl, nw, start=False, stop=False)
            c.pe.matmul(ps_o[0:32, 0:128], zb, qt2, start=False, stop=False)
            c.pe.matmul(ps_o[0:32, 0:128], G0b, qt1n, start=False, stop=False)
            c.i(c.pe.matmul(ps_o[0:32, 0:128], C0b, rmr1, start=False,
                            stop=True), "pe", pes, 1)
            o_done = c.sems["pe"].n
            c.i(c.pe.matmul(ps_su[0:32, 0:32], q1e, vsl, start=True, stop=True),
                "pe", pes, 1)
            c.i(c.pe.matmul(ps_su[0:32, 32:64], k1e, ksl, start=True, stop=True),
                "pe", pes, 1)
            c.i(c.pe.matmul(ps_su[0:32, 64:96], ksl, ksl, start=True, stop=True),
                "pe", pes, 1)
            c.i(c.pe.matmul(ps_su[0:32, 96:128], k1e, utm, start=True,
                            stop=True), "pe", pes, 1)
            su_done = c.sems["pe"].n
            c.w(c.act, "act", pes, su_done)
            c.aop(c.act.copy(stil, ps_su[0:32, 64:96]))
            c.w(c.pe, "pe", acte)
            c.i(c.pe.matmul(ps_su[0:32, 128:160], stil, C0b, start=True,
                            stop=True), "pe", pes, 1)
            stc_done = c.sems["pe"].n
            gC = float(GAMMA ** 128)
            c.w(c.dve, "dve", pes, stc_done)
            c.dop(c.dve.tensor_scalar_mul(G0f, G0f, gC))
            c.dop(c.dve.scalar_tensor_tensor(G0f, ps_su[0:32, 128:160], gC,
                                             G0f, ALU.mult, ALU.add))
            c.dop(c.dve.tensor_tensor(G0f, G0f, ps_su[0:32, 96:128], ALU.add))
            c.dop(c.dve.tensor_scalar_mul(C0f, C0f, gC))
            c.dop(c.dve.tensor_tensor(C0f, C0f, ps_su[0:32, 0:32], ALU.add))
            c.dop(c.dve.tensor_scalar_mul(S0f, S0f, gC))
            c.dop(c.dve.tensor_tensor(S0f, S0f, ps_su[0:32, 32:64], ALU.add))
            c.dop(c.dve.tensor_copy(stb, stt))
            stb_done = c.sems["dv"].n
            c.w(c.pe, "pe", dv, stb_done)
            c.i(c.pe.matmul(ps_su[0:32, 160:192], S0b, C0b, start=True,
                            stop=True), "pe", pes, 1)
            c.w(c.act, "act", pes)
            c.aop(c.act.copy(zb, ps_su[0:32, 160:192]))
            c.w(c.act, "act", pes, o_done)
            c.i(c.act.copy(stage[h % 2][:, ch * 128:(ch + 1) * 128],
                           ps_o[0:32, 0:128]), "act", acte, 1)
        c.w(c.pool, "pool", acte, c.sems["acte"].n)
        c.i(c.pool.dma_start(out=op_dram[h * 32:(h + 1) * 32, :],
                             in_=stage[h % 2]), "pool", stp[h % 2], 16)

    env = {"ps_d": [c.psbank(4), c.psbank(5)], "stp": stp, "op_dram": op_dram}
    emit_outproj_residual(
        c, env, lambda kc, db: wob[:, kc, db * 512:(db + 1) * 512],
        None, xin, xout, opb, opl, 4, False,
        extra_pe_sems=[cst])
    stk.close()






def prep_l1_ffn(inputs):
    wqkv = np.asarray(inputs["L1_ffn_wqkv"], dtype=np.float32)
    qs = np.asarray(inputs["L1_ffn_q_scale"], dtype=np.float32)
    ks = np.asarray(inputs["L1_ffn_k_scale"], dtype=np.float32)
    wo = np.asarray(inputs["L1_ffn_wo"], dtype=np.float32)
    lg = np.asarray(inputs["L1_ffn_ln_g"], dtype=np.float32)
    lb = np.asarray(inputs["L1_ffn_ln_b"], dtype=np.float32)
    w1 = np.asarray(inputs["L1_ffn_w1"], dtype=np.float32)
    b1 = np.asarray(inputs["L1_ffn_b1"], dtype=np.float32)
    w2 = np.asarray(inputs["L1_ffn_w2"], dtype=np.float32)
    b2 = np.asarray(inputs["L1_ffn_b2"], dtype=np.float32)
    HID = 2730
    HIDP = 2816
    w1a = np.zeros((2, 1024, HIDP), dtype=np.float32)
    w1g = np.zeros((2, 1024, HIDP), dtype=np.float32)
    b1a = np.zeros((2, HIDP), dtype=np.float32)
    b1g = np.zeros((2, HIDP), dtype=np.float32)
    w2p = np.zeros((2, HIDP, 1024), dtype=np.float32)
    for l in range(2):
        w1a[l, :, :HID] = w1[l][:, :HID]
        w1g[l, :, :HID] = w1[l][:, HID:]
        b1a[l, :HID] = b1[l][:HID]
        b1g[l, :HID] = b1[l][HID:]
        w2p[l, :HID, :] = w2[l]
    DH, WIN = 64, 512
    half_idx = np.arange(0, DH, 2, dtype=np.float64)
    inv_freq = 1.0 / 10000.0 ** (half_idx / DH)
    t = np.arange(T, dtype=np.float64)
    fr = t[:, None] * inv_freq[None, :]
    fr2 = np.concatenate([fr, fr], axis=1)
    base = (half_idx + 0.4 * DH) / (1.4 * DH)
    power = (t - T // 2) / float(WIN // 2)
    scl = base[None, :] ** power[:, None]
    scl2 = np.concatenate([scl, scl], axis=1)
    cos2, sin2 = np.cos(fr2), np.sin(fr2)
    xpos = np.stack([cos2 * scl2, sin2 * scl2, cos2 / scl2, sin2 / scl2])
    k_idx = np.arange(128)[:, None]
    q_idx = np.arange(512)[None, :]
    dmp = np.stack([np.where(kb * 128 + k_idx >= q_idx, 0.0, -1e30)
                    for kb in range(4)]).astype(np.float32)
    return {
        "l1f_wqkv": np.ascontiguousarray(wqkv.reshape(2, 8, 128, 1536)),
        "l1f_qs": np.stack([bcast_pp(qs[l]) for l in range(2)]),
        "l1f_ks": np.stack([bcast_pp(ks[l]) for l in range(2)]),
        "l1f_wo": np.ascontiguousarray(wo.reshape(2, 4, 128, 1024)),
        "l1f_lg": np.stack([bcast_pp(lg[l]) for l in range(2)]),
        "l1f_lb": np.stack([bcast_pp(lb[l]) for l in range(2)]),
        "l1f_w1a": np.stack([wT_ogtiles(w1a[l]) for l in range(2)]),
        "l1f_w1g": np.stack([wT_ogtiles(w1g[l]) for l in range(2)]),
        "l1f_b1a": np.stack([scales_tile(b1a[l]) for l in range(2)]),
        "l1f_b1g": np.stack([scales_tile(b1g[l]) for l in range(2)]),
        "l1f_w2": np.ascontiguousarray(w2p.reshape(2, 22, 128, 1024)),
        "l1f_b2": np.stack([bcast_pp(b2[l]) for l in range(2)]),
        "l1f_xpos": np.ascontiguousarray(
            xpos.astype(np.float32).reshape(4, 32, 128, 64)
            .transpose(2, 1, 0, 3)),
        "l1f_dmaskp": dmp,
    }


def decl_l1_ffn(nc):
    io = {}
    dt_ = lambda n, shp: nc.dram_tensor(n, shp, F32, kind="ExternalInput").ap()
    io["l1f_wqkv"] = dt_("l1f_wqkv", [2, 8, 128, 1536])
    io["l1f_qs"] = dt_("l1f_qs", [2, 128, 64])
    io["l1f_ks"] = dt_("l1f_ks", [2, 128, 64])
    io["l1f_wo"] = dt_("l1f_wo", [2, 4, 128, 1024])
    io["l1f_lg"] = dt_("l1f_lg", [2, 128, 1024])
    io["l1f_lb"] = dt_("l1f_lb", [2, 128, 1024])
    io["l1f_w1a"] = dt_("l1f_w1a", [2, 22, 1024, 128])
    io["l1f_w1g"] = dt_("l1f_w1g", [2, 22, 1024, 128])
    io["l1f_b1a"] = dt_("l1f_b1a", [2, 128, 22])
    io["l1f_b1g"] = dt_("l1f_b1g", [2, 128, 22])
    io["l1f_w2"] = dt_("l1f_w2", [2, 22, 128, 1024])
    io["l1f_b2"] = dt_("l1f_b2", [2, 128, 1024])
    io["l1f_xpos"] = dt_("l1f_xpos", [128, 32, 4, 64])
    io["l1f_dmaskp"] = dt_("l1f_dmaskp", [4, 128, 512])
    return io


def _l1f_load_shifted(c, z_dram, tt, xt, xsem):
    dv = c.sem("dv")
    c.w(c.sp, "sp", c.sem("sx"))
    c.w(c.sp, "sp", dv, c.sems["dv"].n)
    c.w(c.sp, "sp", c.sems["acte"], c.sems["acte"].n)
    c.i(c.sp.dma_start(out=xt[:, 0:512],
                       in_=z_dram[tt * 128:(tt + 1) * 128, 0:512]),
        "sp", xsem, 16)
    if tt == 0:
        c.i(c.sp.dma_start(out=xt[1:128, 512:1024],
                           in_=z_dram[0:127, 512:1024]), "sp", xsem, 16)
        c.w(c.dve, "dve", xsem)
        c.dop(c.dve.memset(xt[0:1, 512:1024], 0.0))
    else:
        c.i(c.sp.dma_start(out=xt[:, 512:1024],
                           in_=z_dram[tt * 128 - 1:tt * 128 + 127, 512:1024]),
            "sp", xsem, 16)


def emit_l1_ffn(c, xin, xout, io):
    nc = c.nc
    stk = ExitStack()
    sb = lambda n, shp, dt: stk.enter_context(nc.sbuf_tensor(n, shp, dt)).ap()
    cst = c.sem("cst_l1f")
    dv = c.sem("dv")
    pes = c.sem("pe")
    acte = c.sem("acte")
    sx = c.sem("sx")
    stp = [c.sem("st0"), c.sem("st1")]
    opl = [c.sem("op0"), c.sem("op1")]
    wsA = [c.sem("wA0"), c.sem("wA1")]
    wsB = [c.sem("wB0"), c.sem("wB1")]
    ident = c.consts["ident_bf"]
    z_dram = nc.dram_tensor("l1f_z", [T, D], F32, kind="Internal").ap()
    op_dram = nc.dram_tensor("l1f_op", [512, T], BF16, kind="Internal").ap()

    xpos = sb("l1f_xpos_sb", [128, 32, 4, 64], BF16)
    dmk = sb("l1f_dmk_sb", [128, 4, 512], BF16)
    dmkp = sb("l1f_dmkp_sb", [128, 4, 512], BF16)
    opb = [sb(f"l1f_opb{i}", [128, 4, 512], BF16) for i in range(2)]
    c.i(c.pool.dma_start(out=xpos, in_=io["l1f_xpos"]), "pool", cst, 16)
    c.i(c.pool.dma_start(out=dmk, in_=io["dmask"].rearrange("kb p q -> p kb q")),
        "pool", cst, 16)
    c.i(c.pool.dma_start(out=dmkp,
                         in_=io["l1f_dmaskp"].rearrange("kb p q -> p kb q")),
        "pool", cst, 16)

    stat = c.bufs["stat"]
    s_sum, s_sq = stat[:, 0:1], stat[:, 1:2]
    s_mean, s_m2 = stat[:, 2:3], stat[:, 3:4]
    s_bias, s_std, s_rstd = stat[:, 4:5], stat[:, 5:6], stat[:, 6:7]

    def ln_into(xt, dst_f32):
        c.dop(c.dve.reduce_sum(s_sum, xt, axis=AX))
        c.dop(c.dve.tensor_scalar_mul(s_mean, s_sum, 1.0 / D))
        c.dop(c.dve.tensor_tensor(s_m2, s_mean, s_mean, ALU.mult))
        c.i(c.dve.tensor_scalar(s_bias, s_m2, -1.0, 1e-5, ALU.mult, ALU.add),
            "dve", dv, 1)
        c.w(c.act, "act", dv)
        c.aop(c.act.activation(c.bufs["sq"], xt, AF.Square, accum_out=s_sq))
        c.i(c.act.activation(s_std, s_sq, AF.Sqrt, bias=s_bias, scale=1.0 / D),
            "act", acte, 1)
        c.w(c.dve, "dve", acte)
        c.dop(c.dve.reciprocal(s_rstd, s_std))
        c.dop(c.dve.scalar_tensor_tensor(
            dst_f32, xt, s_mean, s_rstd.to_broadcast([128, D]),
            ALU.subtract, ALU.mult))

    for tt in range(32):
        xt = c.bufs["xt"][tt % 2]
        xo = c.bufs["xo"][tt % 2]
        xsem = c.sem(f"xt{tt % 2}")
        c.w(c.sp, "sp", sx)
        c.w(c.sp, "sp", dv, c.sems["dv"].n)
        c.i(c.sp.dma_start(out=xt, in_=xin[tt * 128:(tt + 1) * 128, :]),
            "sp", xsem, 16)
        c.w(c.dve, "dve", xsem)
        c.w(c.act, "act", xsem)
        c.w(c.dve, "dve", pes, c.sems["pe"].n)
        ln_into(xt, xo)
        c.w(c.pool, "pool", dv)
        c.i(c.pool.dma_start(out=z_dram[tt * 128:(tt + 1) * 128, :], in_=xo),
            "pool", sx, 16)

    env = None
    for l in range(2):
        cstl = c.sem(f"cst_l1f_{l}")
        stkA = ExitStack()
        sbA = lambda n, shp, dt: stkA.enter_context(
            nc.sbuf_tensor(n, shp, dt)).ap()
        c.w(c.pool, "pool", acte, c.sems["acte"].n)
        c.w(c.pool, "pool", pes, c.sems["pe"].n)
        c.w(c.pool, "pool", dv, c.sems["dv"].n)
        QT = sbA(f"l1f_qt{l}", [128, 4, T], BF16)
        KT = sbA(f"l1f_kt{l}", [128, 4, T], BF16)
        vt = sbA(f"l1f_vt{l}", [128, 32, 528], BF16)
        wqkvb = sbA(f"l1f_wqkvb{l}", [128, 8, 1536], BF16)
        wob = sbA(f"l1f_wob{l}", [128, 4, 1024], BF16)
        qsb = sbA(f"l1f_qsb{l}", [128, 64], F32)
        ksb = sbA(f"l1f_ksb{l}", [128, 64], F32)
        hbuf = sbA(f"l1f_hbuf{l}", [128, 8, 128], BF16)
        qkvt = sbA(f"l1f_qkvt{l}", [128, 1536], F32)
        qsq = sbA(f"l1f_qsq{l}", [128, 1024], BF16)
        n2 = sbA(f"l1f_n2{l}", [128, 16], F32)
        inv = sbA(f"l1f_inv{l}", [128, 16], F32)
        qr = sbA(f"l1f_qr{l}", [128, 512], BF16)
        kr = sbA(f"l1f_kr{l}", [128, 512], BF16)
        rsc = sbA(f"l1f_rsc{l}", [128, 512], F32)
        pb = [sbA(f"l1f_pb{l}_{i}", [128, 512], BF16) for i in range(2)]
        sbo = [sbA(f"l1f_sbo{l}_{i}", [64, 512], BF16) for i in range(2)]
        cm = sbA(f"l1f_cm{l}", [128, 1], F32)
        tcm = sbA(f"l1f_tcm{l}", [128, 1], F32)
        negc1 = sbA(f"l1f_negc1{l}", [1, 1], F32)
        negc = sbA(f"l1f_negc{l}", [128, 1], F32)
        rd = sbA(f"l1f_rd{l}", [1, 512], F32)
        sbb = sbA(f"l1f_sbb{l}", [64, 512], F32)
        c.i(c.pool.dma_start(out=wqkvb,
                             in_=io["l1f_wqkv"][l].rearrange("kc p o -> p kc o")),
            "pool", cstl, 16)
        c.i(c.pool.dma_start(out=wob,
                             in_=io["l1f_wo"][l].rearrange("kc p o -> p kc o")),
            "pool", cstl, 16)
        c.i(c.pool.dma_start(out=qsb, in_=io["l1f_qs"][l]), "pool", cstl, 16)
        c.i(c.pool.dma_start(out=ksb, in_=io["l1f_ks"][l]), "pool", cstl, 16)
        ps_q = [c.psbank(0), c.psbank(1), c.psbank(2)]
        tp = c.bufs["tp"]
        c.w(c.dve, "dve", cstl)
        c.i(c.dve.memset(vt.rearrange("p tt (h c) -> p tt h c", h=8)[:, :, :, 64:66],
                         1.0), "dve", dv, 1)
        vones = c.sems["dv"].n
        c.w(c.pe, "pe", cstl)
        c.w(c.pe, "pe", c.sem("cst"))
        c.w(c.pe, "pe", cst)
        for tt in range(32):
            xt = c.bufs["xt"][tt % 2]
            xsem = c.sem(f"xt{tt % 2}")
            _l1f_load_shifted(c, z_dram, tt, xt, xsem)
            c.w(c.dve, "dve", xsem)
            c.w(c.dve, "dve", pes, c.sems["pe"].n)
            lt = c.bufs["lt"][tt % 2]
            c.i(c.dve.tensor_copy(lt, xt), "dve", dv, 1)
            c.w(c.pe, "pe", dv)
            for f in range(8):
                c.w(c.pe, "pe", acte)
                c.i(c.pe.transpose(tp[f % 2], lt[:, f * 128:(f + 1) * 128],
                                   ident), "pe", pes, 1)
                c.w(c.act, "act", pes)
                c.i(c.act.copy(hbuf[:, f, :], tp[f % 2]), "act", acte, 1)
            hb_done = c.sems["acte"].n
            c.w(c.pe, "pe", acte, hb_done)
            c.w(c.pe, "pe", dv, c.sems["dv"].n)
            for b3 in range(3):
                last = None
                for kc in range(8):
                    last = c.pe.matmul(ps_q[b3], hbuf[:, kc, :],
                                       wqkvb[:, kc, b3 * 512:(b3 + 1) * 512],
                                       start=(kc == 0), stop=(kc == 7))
                c.i(last, "pe", pes, 1)
            mm_done = c.sems["pe"].n
            c.w(c.act, "act", pes, mm_done)
            for b3 in range(3):
                c.aop(c.act.copy(qkvt[:, b3 * 512:(b3 + 1) * 512], ps_q[b3]))
            ev_done = c.sems["acte"].n
            c.w(c.dve, "dve", acte, ev_done)
            c.dop(c.dve.tensor_tensor(qsq, qkvt[:, 0:1024], qkvt[:, 0:1024],
                                      ALU.mult))
            for h in range(8):
                c.dop(c.dve.reduce_sum(n2[:, h:h + 1],
                                       qsq[:, h * 64:(h + 1) * 64], axis=AX))
                c.dop(c.dve.reduce_sum(n2[:, 8 + h:9 + h],
                                       qsq[:, 512 + h * 64:512 + (h + 1) * 64],
                                       axis=AX))
            c.w(c.act, "act", dv)
            c.aop(c.act.activation(n2, n2, AF.Sqrt))
            c.w(c.dve, "dve", acte)
            c.dop(c.dve.tensor_scalar_max(n2, n2, 1e-12))
            c.dop(c.dve.reciprocal(inv, n2))
            for h in range(8):
                c.dop(c.dve.scalar_tensor_tensor(
                    qsq[:, h * 64:(h + 1) * 64], qkvt[:, h * 64:(h + 1) * 64],
                    inv[:, h:h + 1], qsb, ALU.mult, ALU.mult))
                c.dop(c.dve.scalar_tensor_tensor(
                    qsq[:, 512 + h * 64:512 + (h + 1) * 64],
                    qkvt[:, 512 + h * 64:512 + (h + 1) * 64],
                    inv[:, 8 + h:9 + h], ksb, ALU.mult, ALU.mult))
            xa = xpos[:, tt, 0, :]
            xb = xpos[:, tt, 1, :]
            xc = xpos[:, tt, 2, :]
            xd = xpos[:, tt, 3, :]
            for (srco, dsto, ct, st_) in [(0, qr, xa, xb), (512, kr, xc, xd)]:
                s4 = qsq[:, srco:srco + 512].rearrange(
                    "p (h half j) -> p h half j", h=8, half=2, j=32)
                d4 = dsto.rearrange("p (h half j) -> p h half j", h=8,
                                    half=2, j=32)
                r4 = rsc.rearrange("p (h j) -> p h j", h=8, j=64)
                ctb = ct.unsqueeze(1).to_broadcast([128, 8, 64]).rearrange(
                    "p h (half j) -> p h half j", half=2)
                stb_ = st_.unsqueeze(1).to_broadcast([128, 8, 64]).rearrange(
                    "p h (half j) -> p h half j", half=2)
                r42 = rsc.rearrange("p (h half j) -> p h half j", h=8, half=2,
                                    j=32)
                c.dop(c.dve.tensor_tensor(r42[:, :, 0, :], s4[:, :, 1, :],
                                          stb_[:, :, 0, :], ALU.mult))
                c.dop(c.dve.tensor_tensor(r42[:, :, 1, :], s4[:, :, 0, :],
                                          stb_[:, :, 1, :], ALU.mult))
                c.dop(c.dve.tensor_tensor(
                    d4[:, :, 0, :].rearrange("p h j -> p (h j)")
                    if False else dsto.rearrange(
                        "p (h half j) -> p h half j", h=8, half=2, j=32)[:, :, 0, :],
                    s4[:, :, 0, :], ctb[:, :, 0, :], ALU.mult))
                c.dop(c.dve.tensor_tensor(d4[:, :, 1, :], s4[:, :, 1, :],
                                          ctb[:, :, 1, :], ALU.mult))
                c.dop(c.dve.tensor_tensor(d4[:, :, 0, :], d4[:, :, 0, :],
                                          r42[:, :, 0, :], ALU.subtract))
                c.dop(c.dve.tensor_tensor(d4[:, :, 1, :], d4[:, :, 1, :],
                                          r42[:, :, 1, :], ALU.add))
            rope_done = c.sems["dv"].n
            if tt == 0:
                c.w(c.dve, "dve", dv, vones)
            c.i(c.dve.tensor_copy(
                vt[:, tt, :].rearrange("p (h c) -> p h c", h=8)[:, :, 0:64],
                qkvt[:, 1024:1536].rearrange("p (h j) -> p h j", h=8)),
                "dve", dv, 1)
            c.w(c.pe, "pe", dv, c.sems["dv"].n)
            for f in range(4):
                c.w(c.pe, "pe", acte)
                c.i(c.pe.transpose(tp[f % 2], qr[:, f * 128:(f + 1) * 128],
                                   ident), "pe", pes, 1)
                c.w(c.act, "act", pes)
                c.i(c.act.copy(QT[:, f, tt * 128:(tt + 1) * 128], tp[f % 2]),
                    "act", acte, 1)
            for f in range(4):
                c.w(c.pe, "pe", acte)
                c.i(c.pe.transpose(tp[f % 2], kr[:, f * 128:(f + 1) * 128],
                                   ident), "pe", pes, 1)
                c.w(c.act, "act", pes)
                c.i(c.act.copy(KT[:, f, tt * 128:(tt + 1) * 128], tp[f % 2]),
                    "act", acte, 1)
        qkv_done = c.sems["acte"].n
        env = {"ps_s": [c.psbank(0), c.psbank(1)], "ps_o": c.psbank(2),
               "ps_b": c.psbank(3), "ps_d": [c.psbank(4), c.psbank(5)],
               "pb": pb, "sbo": sbo, "stp": stp, "cm": cm, "tcm": tcm,
               "negc1": negc1, "negc": negc, "rd": rd, "sbb": sbb,
               "op_dram": op_dram}
        for h in range(8):
            base = (h % 2) * 64
            qrows = slice(base, base + 64)
            for qb in range(8):
                nkc = 4 if qb == 0 else 8
                g0 = qb * 4 - (0 if qb == 0 else 4)
                qs_ = slice(qb * 512, (qb + 1) * 512)

                def mask_fn(kc, _qb=qb):
                    if _qb == 0:
                        return dmk[:, kc, :]
                    return dmkp[:, kc, :] if kc < 4 else dmk[:, kc - 4, :]

                emit_attn_hqb(
                    c, env,
                    lambda kc, _g0=g0, _qr=qrows, _h=h:
                        KT[_qr, _h // 2,
                           (_g0 + kc) * 128:(_g0 + kc + 1) * 128],
                    QT[qrows, h // 2, qs_],
                    lambda kc, _g0=g0, _h=h:
                        vt[:, _g0 + kc, _h * 66:_h * 66 + 65],
                    nkc, mask_fn, 8.0,
                    slice(h * 64, (h + 1) * 64), qs_, (h * 8 + qb) % 2,
                    qkv_done)
        emit_outproj_residual(
            c, env, lambda kc, db: wob[:, kc, db * 512:(db + 1) * 512],
            None, z_dram, z_dram, opb, opl, 4, False,
            extra_pe_sems=[cstl])
        c.w(c.pool, "pool", acte, c.sems["acte"].n)
        c.w(c.pool, "pool", pes, c.sems["pe"].n)
        c.w(c.pool, "pool", dv, c.sems["dv"].n)
        stkA.close()
        stkB = ExitStack()
        sbB = lambda n, shp, dt: stkB.enter_context(
            nc.sbuf_tensor(n, shp, dt)).ap()
        hn_fm = sbB(f"l1f_hn{l}", [128, 8, 1024], BF16)
        Hfm = sbB(f"l1f_H{l}", [128, 22, 1024], BF16)
        w2b = sbB(f"l1f_w2b{l}", [128, 22, 1024], BF16)
        lgb = sbB(f"l1f_lgb{l}", [128, 1024], F32)
        lbb = sbB(f"l1f_lbb{l}", [128, 1024], F32)
        b2b = sbB(f"l1f_b2b{l}", [128, 1024], F32)
        b1ab = sbB(f"l1f_b1ab{l}", [128, 22], F32)
        b1gb = sbB(f"l1f_b1gb{l}", [128, 22], F32)
        w1bufs = [sbB(f"l1f_w1b{l}_{i}", [128, 8, 128], BF16) for i in range(2)]
        w1bufs2 = [sbB(f"l1f_w1c{l}_{i}", [128, 8, 128], BF16) for i in range(2)]
        tmpA = [sbB(f"l1f_tA{l}_{i}", [128, 1024], BF16) for i in range(2)]
        tmpB = [sbB(f"l1f_tB{l}_{i}", [128, 1024], BF16) for i in range(2)]
        cstf = c.sem(f"cst_l1ff_{l}")
        c.i(c.pool.dma_start(out=lgb, in_=io["l1f_lg"][l]), "pool", cstf, 16)
        c.i(c.pool.dma_start(out=lbb, in_=io["l1f_lb"][l]), "pool", cstf, 16)
        c.i(c.pool.dma_start(out=b2b, in_=io["l1f_b2"][l]), "pool", cstf, 16)
        c.i(c.pool.dma_start(out=b1ab, in_=io["l1f_b1a"][l]), "pool", cstf, 16)
        c.i(c.pool.dma_start(out=b1gb, in_=io["l1f_b1g"][l]), "pool", cstf, 16)
        c.i(c.pool.dma_start(out=w2b,
                             in_=io["l1f_w2"][l].rearrange("kc p o -> p kc o")),
            "pool", cstf, 16)
        c.sb[f"ps_l1f{l}"] = c.ps8[:, 0:4, :].rearrange("p (a b) f -> p a b f",
                                                        b=2)
        ps_y = [c.psbank(4), c.psbank(5)]
        tp = c.bufs["tp"]
        for q4 in range(4):
            toff = q4 * 8
            c.w(c.dve, "dve", cstf)
            for i in range(8):
                tt = toff + i
                xt = c.bufs["xt"][i % 2]
                xsem = c.sem(f"xt{i % 2}")
                _l1f_load_shifted(c, z_dram, tt, xt, xsem)
                c.w(c.dve, "dve", xsem)
                c.w(c.act, "act", xsem)
                c.w(c.dve, "dve", acte, c.sems["acte"].n)
                c.w(c.dve, "dve", pes, c.sems["pe"].n)
                ln_into(xt, c.bufs["xo"][i % 2])
                lt = c.bufs["lt"][i % 2]
                c.dop(c.dve.tensor_tensor(c.bufs["xo"][i % 2],
                                          c.bufs["xo"][i % 2], lgb, ALU.mult))
                c.i(c.dve.tensor_tensor(lt, c.bufs["xo"][i % 2], lbb, ALU.add),
                    "dve", dv, 1)
                c.w(c.pe, "pe", dv)
                for f in range(8):
                    c.w(c.pe, "pe", acte)
                    c.i(c.pe.transpose(tp[f % 2], lt[:, f * 128:(f + 1) * 128],
                                       ident), "pe", pes, 1)
                    c.w(c.act, "act", pes)
                    c.i(c.act.copy(hn_fm[:, f, i * 128:(i + 1) * 128],
                                   tp[f % 2]), "act", acte, 1)
            emit_ab_fused(
                c, f"ps_l1f{l}", 1024, 2816, 1024,
                io["l1f_w1g"][l], io["l1f_w1a"][l],
                lambda kc, tb: hn_fm[:, kc, tb * 512:(tb + 1) * 512],
                lambda og: Hfm[:, og, :],
                lambda og: 1.0, lambda og: 1.0,
                AF.Gelu, w1bufs, w1bufs2, wsA, wsB, tmpA, tmpB,
                biasA_fn=lambda og: b1gb[:, og:og + 1],
                biasB_fn=lambda og: b1ab[:, og:og + 1],
                extra_pe_waits=[(c.pe, "pe", cstf, None)])
            for i in range(8):
                tt = toff + i
                xo = c.bufs["xo"][i % 2]
                xsem = c.sem(f"xo{i % 2}")
                c.w(c.sp, "sp", sx)
                c.w(c.sp, "sp", dv, c.sems["dv"].n)
                c.i(c.sp.dma_start(out=xo, in_=z_dram[tt * 128:(tt + 1) * 128, :]),
                    "sp", xsem, 16)
                for db in range(2):
                    c.w(c.pe, "pe", acte)
                    c.w(c.pe, "pe", dv, c.sems["dv"].n)
                    last = None
                    for kc in range(22):
                        last = c.pe.matmul(
                            ps_y[db], Hfm[:, kc, i * 128:(i + 1) * 128],
                            w2b[:, kc, db * 512:(db + 1) * 512],
                            start=(kc == 0), stop=(kc == 21))
                    c.i(last, "pe", pes, 1)
                mm_done = c.sems["pe"].n
                c.w(c.dve, "dve", xsem)
                c.w(c.dve, "dve", pes, mm_done)
                c.dop(c.dve.tensor_tensor(xo, xo, b2b, ALU.add))
                c.i(c.dve.tensor_tensor(xo[:, 0:512], xo[:, 0:512], ps_y[0],
                                        ALU.add), "dve", dv, 1)
                c.i(c.dve.tensor_tensor(xo[:, 512:1024], xo[:, 512:1024],
                                        ps_y[1], ALU.add), "dve", dv, 1)
                c.w(c.pool, "pool", dv)
                c.i(c.pool.dma_start(out=z_dram[tt * 128:(tt + 1) * 128, :],
                                     in_=xo), "pool", sx, 16)
        c.w(c.pool, "pool", acte, c.sems["acte"].n)
        c.w(c.pool, "pool", pes, c.sems["pe"].n)
        c.w(c.pool, "pool", dv, c.sems["dv"].n)
        stkB.close()

    for tt in range(32):
        xt = c.bufs["xt"][tt % 2]
        xo = c.bufs["xo"][tt % 2]
        xsem = c.sem(f"xt{tt % 2}")
        xsem2 = c.sem(f"xo{tt % 2}")
        c.w(c.sp, "sp", sx)
        c.w(c.sp, "sp", dv, c.sems["dv"].n)
        c.i(c.sp.dma_start(out=xt, in_=xin[tt * 128:(tt + 1) * 128, :]),
            "sp", xsem, 16)
        c.i(c.sp.dma_start(out=xo, in_=z_dram[tt * 128:(tt + 1) * 128, :]),
            "sp", xsem2, 16)
        c.w(c.dve, "dve", xsem)
        c.w(c.dve, "dve", xsem2)
        c.i(c.dve.tensor_tensor(xo, xo, xt, ALU.add), "dve", dv, 1)
        c.w(c.pool, "pool", dv)
        c.i(c.pool.dma_start(out=xout[tt * 128:(tt + 1) * 128, :], in_=xo),
            "pool", sx, 16)
    stk.close()



SUBBLOCKS = ["l0_attn", "l0_ffn", "l1_attn", "l1_ffn", "l2_attn", "l2_ffn",
             "l3_attn", "l3_ffn"]
PREPS = {"l0_ffn": prep_l0_ffn, "l0_attn": prep_l0_attn, "l1_attn": prep_l1_attn,
         "l2_ffn": prep_l2_ffn, "l3_ffn": prep_l3_ffn, "l2_attn": prep_l2_attn,
         "l3_attn": prep_l3_attn, "l1_ffn": prep_l1_ffn}
DECLS = {"l0_ffn": decl_l0_ffn, "l0_attn": decl_l0_attn, "l1_attn": decl_l1_attn,
         "l2_ffn": decl_l2_ffn, "l3_ffn": decl_l3_ffn, "l2_attn": decl_l2_attn,
         "l3_attn": decl_l3_attn, "l1_ffn": decl_l1_ffn}
EMITS = {"l0_ffn": emit_l0_ffn, "l0_attn": emit_l0_attn, "l1_attn": emit_l1_attn,
         "l2_ffn": emit_l2_ffn, "l3_ffn": emit_l3_ffn, "l2_attn": emit_l2_attn,
         "l3_attn": emit_l3_attn, "l1_ffn": emit_l1_ffn}


def build_program(blocks):
    nc = bass.Bass("TRN2", target_bir_lowering=False)
    _orig_dram = nc.dram_tensor
    def _dram(name, shape, dt, **kw):
        return _orig_dram(name, shape,
                          BF16 if name in BF16_DRAM else dt, **kw)
    nc.dram_tensor = _dram
    stack = ExitStack()
    xin = nc.dram_tensor("x_in", [T, D], F32, kind="ExternalInput").ap()
    xout = nc.dram_tensor("x_out", [T, D], F32, kind="ExternalOutput").ap()
    io = {
        "ident": nc.dram_tensor("ident", [128, 128], F32,
                                kind="ExternalInput").ap(),
        "ones_row": nc.dram_tensor("ones_row", [1, 128], F32,
                                   kind="ExternalInput").ap(),
        "dmask": nc.dram_tensor("dmask", [4, 128, 512], F32,
                                kind="ExternalInput").ap(),
    }
    for b in blocks:
        io.update(DECLS[b](nc))
    c = C(nc, stack)
    c.ps8 = nc.alloc_psum_tensor("ps8", [128, 8, 512], F32).ap()
    sb = lambda n, shp, dt: nc.alloc_sbuf_tensor(n, shp, dt).ap()
    c.bufs = {
        "xt": [sb(f"sh_xt{i}", [128, D], F32) for i in range(2)],
        "lt": [sb(f"sh_lt{i}", [128, D], BF16) for i in range(2)],
        "xo": [sb(f"sh_xo{i}", [128, D], F32) for i in range(2)],
        "sq": sb("sh_sq", [128, D], BF16),
        "stat": sb("sh_stat", [128, 16], F32),
        "tp": [c.psbank_bf(6), c.psbank_bf(7)],
    }
    scr = [nc.dram_tensor(f"x_scr{i}", [T, D], F32, kind="Internal").ap()
           for i in range(2)] if len(blocks) > 1 else []
    emit_consts(c, io)
    for bi, b in enumerate(blocks):
        bin_ = xin if bi == 0 else scr[(bi - 1) % 2]
        bout = xout if bi == len(blocks) - 1 else scr[bi % 2]
        EMITS[b](c, bin_, bout, io)
    sx = c.sem("sx")
    c.w(c.pool, "pool", sx)
    c.check_deadlock()
    stack.close()
    return nc


def host_constants():
    dmask = np.zeros((4, 128, 512), dtype=np.float32)
    for kb in range(4):
        k_idx = kb * 128 + np.arange(128)[:, None]
        q_idx = np.arange(512)[None, :]
        dmask[kb] = np.where(k_idx <= q_idx, 0.0, -1e30)
    return {
        "ident": np.eye(128, dtype=np.float32),
        "ones_row": np.ones((1, 128), dtype=np.float32),
        "dmask": dmask,
    }


def prep_shared(inputs, blocks):
    shared = host_constants()
    for b in blocks:
        shared.update(PREPS[b](inputs))
    for n in list(shared):
        if n in BF16_DRAM:
            shared[n] = shared[n].astype(BF)
    return shared


def run_blocks(blocks, x, inputs, n_cores=NCORES, trace=False):
    shared = prep_shared(inputs, blocks)
    nc = build_program(blocks)
    in_maps = []
    for core in range(n_cores):
        m = dict(shared)
        m["x_in"] = np.ascontiguousarray(x[core], dtype=np.float32)
        in_maps.append(m)
    res = run_bass_kernel_spmd(nc, in_maps, core_ids=list(range(n_cores)),
                               trace=trace)
    out = np.stack([r["x_out"] for r in res.results], axis=0)
    return out, res


def kernel(**inputs):
    x = np.ascontiguousarray(inputs["L0_attn_x"], dtype=np.float32)
    out, _ = run_blocks(SUBBLOCKS, x, inputs)
    return out
